# version 01
# baseline (speedup 1.0000x reference)
import numpy as np
from contextlib import ExitStack, contextmanager
import concourse.bass as bass
import concourse.mybir as mybir
from concourse.bass_utils import run_bass_kernel_spmd
import ml_dtypes

F32 = mybir.dt.float32
BF16 = mybir.dt.bfloat16
F16 = mybir.dt.float16
I32 = mybir.dt.int32
U8 = mybir.dt.uint8
AF = mybir.ActivationFunctionType
ALU = mybir.AluOpType
AX = mybir.AxisListType
bf16 = ml_dtypes.bfloat16

S, D = 2048, 4096
NT = S // 128
ND = D // 128
NCORES = 8
NEG_BIG = -1.0e30
SELF_SYNC = True
DEBUG_STEPS = False


def ceil_div(a, b):
    return (a + b - 1) // b

import os
XBAR = os.environ.get("KBAR", "")

ENG_OF = {"T": "tensor", "V": "vector", "A": "scalar", "G": "gpsimd",
          "S": "sync", "GD": "gpsimd", "AD": "scalar",
          "Sf": "sync", "GDf": "gpsimd", "ADf": "scalar"}
DMA_KINDS = ("S", "GD", "AD", "Sf", "GDf", "ADf")
QUEUES = ("S", "GD", "AD")
PHYS = ("tensor", "vector", "scalar", "sync", "gpsimd")
_ESZ = {F32: 4, BF16: 2, F16: 2, I32: 4, U8: 1}


import traceback


class Step:
    __slots__ = ("kind", "eng", "fn", "deps", "inc", "kidx", "pos", "where")

    def __init__(self, kind, eng, fn):
        self.kind, self.eng, self.fn = kind, eng, fn
        self.where = traceback.extract_stack()[-3].lineno if DEBUG_STEPS else 0
        self.deps = []
        self.inc = kind in DMA_KINDS
        self.kidx = None
        self.pos = None


class Buf:
    __slots__ = ("h", "name", "shape", "dtype", "space", "esz", "rows", "fbytes", "log")

    def __init__(self, h, name, shape, dtype, space):
        self.h, self.name, self.shape, self.dtype, self.space = h, name, tuple(shape), dtype, space
        self.esz = _ESZ[dtype]
        self.rows = self.shape[0]
        self.fbytes = (int(np.prod(self.shape[1:], dtype=np.int64)) * self.esz
                       if len(self.shape) > 1 else self.esz)
        self.log = []

    def ap(self):
        return self.h.ap() if hasattr(self.h, "ap") else self.h[:]

    def __getitem__(self, key):
        return self.ap().__getitem__(key)

    def box(self, ps=None, fs=None):
        p0, p1 = (0, self.rows) if ps is None else ps
        if fs is None:
            f0, f1 = 0, self.fbytes
        else:
            f0, f1 = fs[0] * self.esz, fs[1] * self.esz
        if self.space == "psum":
            f0 = (f0 // 2048) * 2048
            f1 = ((f1 + 2047) // 2048) * 2048
        return (self, p0, p1, f0, f1)


class Prog:
    def __init__(self, nc, ctx):
        self.nc = nc
        self.ctx = ctx
        self.stacks = [ctx]
        self.streams = {e: [] for e in PHYS}
        self.kcount = {k: [] for k in ENG_OF}
        self.hw = {e: {k: 0 for k in ENG_OF} for e in PHYS}
        self.floor = {e: [] for e in PHYS}
        self.bufs = []
        self.sems = {}
        self.uid = 0
        self.markers = {k: (0, 0) for k in QUEUES}
        self.last_step = {k: None for k in ENG_OF}
        self.dmy = Buf(ctx.enter_context(
            nc.sbuf_tensor("dmyq", [128, 8], U8)), "dmyq", [128, 8], U8, "sbuf")

    def _nm(self, name):
        self.uid += 1
        return f"{name}_{self.uid}"

    def dram(self, name, shape, dtype, kind="Internal"):
        h = self.nc.dram_tensor(self._nm(name), list(shape), dtype, kind=kind)
        b = Buf(h, name, shape, dtype, "dram")
        self.bufs.append(b)
        return b

    def io(self, name, shape, dtype, kind):
        h = self.nc.dram_tensor(name, list(shape), dtype, kind=kind)
        b = Buf(h, name, shape, dtype, "dram")
        self.bufs.append(b)
        return b

    def sbuf(self, name, shape, dtype):
        h = self.stacks[-1].enter_context(self.nc.sbuf_tensor(self._nm(name), list(shape), dtype))
        b = Buf(h, name, shape, dtype, "sbuf")
        self.bufs.append(b)
        return b

    def psum(self, name, shape, dtype=F32):
        h = self.stacks[-1].enter_context(self.nc.psum_tensor(self._nm(name), list(shape), dtype))
        b = Buf(h, name, shape, dtype, "psum")
        self.bufs.append(b)
        return b

    @contextmanager
    def phase(self, name=""):
        st = ExitStack()
        self.stacks.append(st)
        try:
            yield
        finally:
            self.barrier()
            st.close()
            self.stacks.pop()

    def _flush(self, kind, need_kidx):
        cover, ordn = self.markers[kind]
        if cover >= need_kidx:
            return (kind + "f", 16 * ordn)
        fk = kind + "f"
        eng = ENG_OF[kind]
        st = Step(fk, eng, lambda e: e.dma_start(out=self.dmy[:, 4:8], in_=self.dmy[:, 0:4]))
        st.pos = len(self.streams[eng])
        st.kidx = len(self.kcount[fk]) + 1
        self.kcount[fk].append(st)
        if st.kidx > 1:
            st.deps.append((fk, 16 * (st.kidx - 1)))
        self.streams[eng].append(st)
        self.last_step[fk] = st
        self.markers[kind] = (len(self.kcount[kind]), st.kidx)
        return (fk, 16 * st.kidx)

    def _event_of(self, step):
        if step.kind in DMA_KINDS:
            return (step.kind, 16 * step.kidx)
        for s in self.streams[step.eng][step.pos:]:
            if s.kind == step.kind and s.inc:
                return (step.kind, s.kidx)
        step.inc = True
        step.kidx = len(self.kcount[step.kind]) + 1
        self.kcount[step.kind].append(step)
        return (step.kind, step.kidx)

    def emit(self, kind, fn, rd=(), wr=()):
        eng = ENG_OF[kind]
        step = Step(kind, eng, fn)
        step.pos = len(self.streams[eng])
        if kind in DMA_KINDS:
            step.kidx = len(self.kcount[kind]) + 1
            self.kcount[kind].append(step)
        rd = [r.box() if isinstance(r, Buf) else r for r in rd]
        wr = [w.box() if isinstance(w, Buf) else w for w in wr]
        deps = set()
        for (buf, p0, p1, f0, f1) in rd:
            for (q0, q1, g0, g1, st2, w) in buf.log:
                if w and q0 < p1 and p0 < q1 and g0 < f1 and f0 < g1:
                    deps.add(st2)
        for (buf, p0, p1, f0, f1) in wr:
            for (q0, q1, g0, g1, st2, w) in buf.log:
                if q0 < p1 and p0 < q1 and g0 < f1 and f0 < g1:
                    deps.add(st2)
        events = []
        for dst in deps:
            if dst is step:
                continue
            if dst.kind in DMA_KINDS:
                if dst.eng == eng:
                    events.append((dst.kind, 16 * dst.kidx))
                else:
                    events.append(self._flush(dst.kind, dst.kidx))
                continue
            if (not SELF_SYNC) and dst.eng == eng:
                continue
            events.append(self._event_of(dst))
        if self.floor[eng]:
            events.extend(self.floor[eng])
            self.floor[eng] = []
        hw = self.hw[eng]
        for (k, v) in events:
            if v > hw[k]:
                hw[k] = v
                step.deps.append((k, v))
        self.streams[eng].append(step)
        self.last_step[kind] = step
        for (buf, p0, p1, f0, f1) in wr:
            buf.log = [e for e in buf.log
                       if not (p0 <= e[0] and e[1] <= p1 and f0 <= e[2] and e[3] <= f1)]
            buf.log.append((p0, p1, f0, f1, step, True))
        for (buf, p0, p1, f0, f1) in rd:
            buf.log = [e for e in buf.log
                       if not (e[5] is False and e[4].eng == eng and p0 <= e[0]
                               and e[1] <= p1 and f0 <= e[2] and e[3] <= f1)]
            buf.log.append((p0, p1, f0, f1, step, False))
        return step

    def barrier(self):
        events = []
        for k in ENG_OF:
            if k in DMA_KINDS:
                if self.kcount[k]:
                    events.append((k, 16 * len(self.kcount[k])))
            elif self.last_step[k] is not None:
                events.append(self._event_of(self.last_step[k]))
        for e in PHYS:
            cur = dict(self.floor[e])
            for (k, v) in events:
                if cur.get(k, -1) < v:
                    cur[k] = v
            self.floor[e] = list(cur.items())
        for b in self.bufs:
            b.log = []

    def tap_sbuf(self, buf, name):
        t = self.io(name, list(buf.shape), buf.dtype, "ExternalOutput")
        self.emit("GD", lambda e: e.dma_start(out=t.ap(), in_=buf.ap()), rd=[buf], wr=[t])
        return t

    def tap(self, buf, name):
        t = self.io(name, list(buf.shape), buf.dtype, "ExternalOutput")
        rows = buf.rows
        step = max(1, (2 << 20) // max(1, buf.fbytes))
        for r0 in range(0, rows, step):
            r1 = min(rows, r0 + step)
            self.emit("GD", lambda e, r0=r0, r1=r1: e.dma_start(
                out=t.ap()[r0:r1], in_=buf.ap()[r0:r1]),
                rd=[buf.box(ps=(r0, r1))], wr=[t.box(ps=(r0, r1))])
        return t

    def finalize(self):
        for k in ENG_OF:
            if self.kcount[k]:
                self.sems[k] = self.ctx.enter_context(self.nc.semaphore(f"sm{k}"))
        block = self.ctx.enter_context(self.nc.Block())
        prog = self

        def runner(eng_name):
            def run(eng):
                hw = {k: 0 for k in ENG_OF}
                for st in prog.streams[eng_name]:
                    for (k, v) in st.deps:
                        if v > hw[k]:
                            hw[k] = v
                            eng.wait_ge(prog.sems[k], v)
                    ins = st.fn(eng)
                    if st.inc:
                        assert ins is not None, f"step kind={st.kind} line={st.where} returned None"
                        ins.then_inc(prog.sems[st.kind], 16 if st.kind in DMA_KINDS else 1)
                for k in DMA_KINDS:
                    if ENG_OF[k] == eng_name and prog.kcount[k]:
                        eng.wait_ge(prog.sems[k], 16 * len(prog.kcount[k]))
            return run

        block.tensor(runner("tensor"))
        block.vector(runner("vector"))
        block.scalar(runner("scalar"))
        block.sync(runner("sync"))
        block.gpsimd(runner("gpsimd"))



def emit_build_xT(p, x_in, xT_h, identb):
    xT_v = xT_h.ap().rearrange("(c d) t -> d c t", d=128)
    with p.phase("bxT"):
        xf = [p.sbuf("bxf", [128, D], F32) for _ in range(2)]
        xb = [p.sbuf("bxb", [128, D], BF16) for _ in range(4)]
        pst = [p.psum("bxp", [128, 512], BF16) for _ in range(2)]
        stg = [p.sbuf("bxs", [128, 512], BF16) for _ in range(2)]
        for sup in range(4):
            for u in range(4):
                i = sup * 4 + u
                f = xf[i % 2]
                b = xb[u]
                p.emit("S", lambda e, i=i, f=f: e.dma_start(out=f[:], in_=x_in[128 * i:128 * (i + 1), :]),
                       rd=[x_in.box(ps=(128 * i, 128 * (i + 1)))], wr=[f])
                p.emit("V", lambda e, f=f, b=b: e.tensor_copy(b[:], f[:]), rd=[f], wr=[b])
            for j in range(ND):
                ps = pst[j % 2]
                sg = stg[j % 2]

                def tr(e, j=j, ps=ps):
                    ins = None
                    for u in range(4):
                        ins = e.transpose(ps[:, 128 * u:128 * (u + 1)], xb[u][:, 128 * j:128 * (j + 1)],
                                          identb[:])
                    return ins
                p.emit("T", tr, rd=[xb[0], xb[1], xb[2], xb[3], identb], wr=[ps])
                p.emit("V", lambda e, ps=ps, sg=sg: e.tensor_copy(sg[:], ps[:]), rd=[ps], wr=[sg])
                p.emit("GD", lambda e, sg=sg, j=j, sup=sup: e.dma_start(
                    out=xT_v[:, j, 512 * sup:512 * (sup + 1)], in_=sg[:]),
                    rd=[sg], wr=[xT_h.box(ps=(128 * j, 128 * (j + 1)), fs=(512 * sup, 512 * (sup + 1)))])


def emit_ln_residual(p, sub_h, x_in, x_out, sub_dtype=BF16, extra_row=None):
    with p.phase("ln"):
        sb = [p.sbuf("ln_s", [128, D], sub_dtype) for _ in range(2)]
        xf = [p.sbuf("ln_x", [128, D], F32) for _ in range(2)]
        sm = [p.sbuf("ln_sum", [128, D], F32) for _ in range(2)]
        dump = p.sbuf("ln_d", [128, D], F32)
        on = [p.sbuf("ln_o", [128, D], F32) for _ in range(2)]
        cols = p.sbuf("ln_c", [128, 8], F32)
        epsc = p.sbuf("ln_e", [128, 1], F32)
        p.emit("V", lambda e: e.memset(epsc[:], 1e-5), wr=[epsc])
        for i in range(NT):
            b, xi, si, oi = sb[i % 2], xf[i % 2], sm[i % 2], on[i % 2]
            rowsl = (128 * i, 128 * (i + 1))
            p.emit("S", lambda e, b=b, i=i: e.dma_start(out=b[:], in_=sub_h[128 * i:128 * (i + 1), :]),
                   rd=[sub_h.box(ps=rowsl)], wr=[b])
            p.emit("AD", lambda e, xi=xi, i=i: e.dma_start(out=xi[:], in_=x_in[128 * i:128 * (i + 1), :]),
                   rd=[x_in.box(ps=rowsl)], wr=[xi])
            p.emit("V", lambda e, b=b, xi=xi, si=si: e.tensor_add(si[:], xi[:], b[:]),
                   rd=[b, xi], wr=[si])
            if extra_row is not None:
                p.emit("V", lambda e, si=si: e.tensor_add(si[:], si[:], extra_row[:]),
                       rd=[si, extra_row], wr=[si])
            p.emit("V", lambda e, si=si: e.reduce_sum(cols[:, 0:1], si[:], axis=AX.X),
                   rd=[si], wr=[cols.box(fs=(0, 1))])
            p.emit("V", lambda e, si=si: e.scalar_tensor_tensor(
                dump[:], si[:], 1.0, si[:], ALU.mult, ALU.mult, accum_out=cols[:, 1:2]),
                rd=[si], wr=[dump, cols.box(fs=(1, 2))])
            p.emit("V", lambda e: e.tensor_scalar_mul(cols[:, 2:3], cols[:, 0:1], 1.0 / D),
                   rd=[cols.box(fs=(0, 1))], wr=[cols.box(fs=(2, 3))])
            p.emit("V", lambda e: e.tensor_scalar_mul(cols[:, 3:4], cols[:, 1:2], 1.0 / D),
                   rd=[cols.box(fs=(1, 2))], wr=[cols.box(fs=(3, 4))])
            p.emit("V", lambda e: e.scalar_tensor_tensor(
                cols[:, 4:5], cols[:, 2:3], -1.0, cols[:, 2:3], ALU.mult, ALU.mult),
                rd=[cols.box(fs=(2, 3))], wr=[cols.box(fs=(4, 5))])
            p.emit("V", lambda e: e.tensor_add(cols[:, 3:4], cols[:, 3:4], cols[:, 4:5]),
                   rd=[cols.box(fs=(3, 5))], wr=[cols.box(fs=(3, 4))])
            p.emit("A", lambda e: e.activation(cols[:, 5:6], cols[:, 3:4], AF.Ln, bias=epsc[:]),
                   rd=[cols.box(fs=(3, 4)), epsc], wr=[cols.box(fs=(5, 6))])
            p.emit("A", lambda e: e.activation(cols[:, 6:7], cols[:, 5:6], AF.Exp, scale=-0.5),
                   rd=[cols.box(fs=(5, 6))], wr=[cols.box(fs=(6, 7))])
            p.emit("V", lambda e: e.scalar_tensor_tensor(
                cols[:, 7:8], cols[:, 2:3], -1.0, cols[:, 6:7], ALU.mult, ALU.mult),
                rd=[cols.box(fs=(2, 3)), cols.box(fs=(6, 7))], wr=[cols.box(fs=(7, 8))])
            p.emit("A", lambda e, si=si, oi=oi: e.activation(
                oi[:], si[:], AF.Identity, bias=cols[:, 7:8], scale=cols[:, 6:7]),
                rd=[si, cols.box(fs=(6, 8))], wr=[oi])
            p.emit("GD", lambda e, oi=oi, i=i: e.dma_start(out=x_out[128 * i:128 * (i + 1), :], in_=oi[:]),
                   rd=[oi], wr=[x_out.box(ps=rowsl)])


def emit_dora(p, x_in, wt, at, btt, mag, xT_h, sub_h, identb):
    emit_build_xT(p, x_in, xT_h, identb)

    at_sb = p.sbuf("do_at", [128, 32 * 32], BF16)
    bt_bf = p.sbuf("do_btb", [32, D], BF16)
    xa_sb = p.sbuf("do_xa", [32, S], BF16)
    p.dbg_xa = xa_sb
    c_rep = p.sbuf("do_crep", [128, D], F32)
    c_row = p.sbuf("do_crow", [1, D], F32)
    p.dbg_c_row = c_row; p.dbg_crep = c_rep
    ones_b = p.sbuf("do_onb", [128, 1], BF16)
    p.emit("GD", lambda e: e.dma_start(out=at_sb[:], in_=at.ap().rearrange("(c p) r -> p c r", p=128)),
           rd=[at], wr=[at_sb])
    p.emit("GD", lambda e: e.dma_start(out=bt_bf[:], in_=btt.ap()), rd=[btt], wr=[bt_bf])
    p.emit("V", lambda e: e.memset(ones_b[:], 1.0), wr=[ones_b])

    with p.phase("doxa"):
        wtp = [p.sbuf("do_pan", [128, D], BF16) for _ in range(2)]
        ps_xa = p.psum("do_psxa", [32, S], F32)
        for i in range(ND):
            pan = wtp[i % 2]
            p.emit("GD", lambda e, pan=pan, i=i: e.dma_start(
                out=pan[:, 0:S], in_=xT_h[128 * i:128 * (i + 1), :]),
                rd=[xT_h.box(ps=(128 * i, 128 * (i + 1)))], wr=[pan.box(fs=(0, S))])

            def mm(e, pan=pan, i=i):
                ins = None
                for sp in range(4):
                    ins = e.matmul(ps_xa[:, 512 * sp:512 * (sp + 1)], at_sb[:, 32 * i:32 * (i + 1)],
                                   pan[:, 512 * sp:512 * (sp + 1)], start=(i == 0), stop=(i == ND - 1))
                return ins
            p.emit("T", mm, rd=[pan.box(fs=(0, S)), at_sb.box(fs=(32 * i, 32 * (i + 1)))], wr=[ps_xa])
        p.emit("V", lambda e: e.tensor_copy(xa_sb[:], ps_xa[:]), rd=[ps_xa], wr=[xa_sb])

    with p.phase("docn"):
        wtp = [p.sbuf("do_pan", [128, D], BF16) for _ in range(2)]
        sq = p.sbuf("do_sq", [128, D], BF16)
        ps_row = p.psum("do_psrow", [32, D], F32)
        rowp = p.sbuf("do_rows", [1, 3 * D], F32)
        wat_f = p.sbuf("do_wat", [32, D], F32)
        gb_f = p.sbuf("do_gb", [32, D], F32)
        bt_f = p.sbuf("do_btf", [32, D], F32)
        g_bf = p.sbuf("do_g", [32, 32], BF16)
        tmp = p.sbuf("do_tmp", [32, D], F32)
        ones32f = p.sbuf("do_on32", [32, 1], F32)
        p.emit("AD", lambda e: e.dma_start(out=bt_f[:], in_=btt.ap()), rd=[btt], wr=[bt_f])
        p.emit("V", lambda e: e.memset(ones32f[:], 1.0), wr=[ones32f])
        for i in range(ND):
            pan = wtp[i % 2]
            p.emit("GD", lambda e, pan=pan, i=i: e.dma_start(
                out=pan[:], in_=wt[128 * i:128 * (i + 1), :]),
                rd=[wt.box(ps=(128 * i, 128 * (i + 1)))], wr=[pan])
            p.emit("A", lambda e, pan=pan: e.activation(sq[:], pan[:], AF.Square),
                   rd=[pan], wr=[sq])

            def mm(e, i=i):
                ins = None
                for sp in range(8):
                    ins = e.matmul(ps_row[0:1, 512 * sp:512 * (sp + 1)], ones_b[:],
                                   sq[:, 512 * sp:512 * (sp + 1)], start=(i == 0), stop=(i == ND - 1))
                return ins
            p.emit("T", mm, rd=[sq, ones_b], wr=[ps_row])
        p.emit("V", lambda e: e.tensor_copy(rowp[0:1, 0:D], ps_row[0:1, :]), rd=[ps_row],
               wr=[rowp.box(fs=(0, D))])
        p.barrier()
        for i in range(ND):
            pan = wtp[i % 2]
            p.emit("GD", lambda e, pan=pan, i=i: e.dma_start(
                out=pan[:], in_=wt[128 * i:128 * (i + 1), :]),
                rd=[wt.box(ps=(128 * i, 128 * (i + 1)))], wr=[pan])

            def mm(e, pan=pan, i=i):
                ins = None
                for sp in range(8):
                    ins = e.matmul(ps_row[:, 512 * sp:512 * (sp + 1)], at_sb[:, 32 * i:32 * (i + 1)],
                                   pan[:, 512 * sp:512 * (sp + 1)], start=(i == 0), stop=(i == ND - 1))
                return ins
            p.emit("T", mm, rd=[pan, at_sb.box(fs=(32 * i, 32 * (i + 1)))], wr=[ps_row])
        p.emit("V", lambda e: e.tensor_copy(wat_f[:], ps_row[:]), rd=[ps_row], wr=[wat_f])
        p.barrier()

        def mmg(e):
            ins = None
            for i in range(ND):
                ins = e.matmul(ps_row[0:32, 0:32], at_sb[:, 32 * i:32 * (i + 1)],
                               at_sb[:, 32 * i:32 * (i + 1)], start=(i == 0), stop=(i == ND - 1))
            return ins
        p.emit("T", mmg, rd=[at_sb], wr=[ps_row.box(fs=(0, 512))])
        p.emit("V", lambda e: e.tensor_copy(g_bf[:], ps_row[0:32, 0:32]),
               rd=[ps_row.box(fs=(0, 512))], wr=[g_bf])
        p.barrier()

        def mmgb(e):
            ins = None
            for sp in range(8):
                ins = e.matmul(ps_row[:, 512 * sp:512 * (sp + 1)], g_bf[:],
                               bt_bf[:, 512 * sp:512 * (sp + 1)], start=True, stop=True)
            return ins
        p.emit("T", mmgb, rd=[g_bf, bt_bf], wr=[ps_row])
        p.emit("V", lambda e: e.tensor_copy(gb_f[:], ps_row[:]), rd=[ps_row], wr=[gb_f])
        p.barrier()
        p.emit("V", lambda e: e.tensor_mul(tmp[:], bt_f[:], wat_f[:]), rd=[bt_f, wat_f], wr=[tmp])

        def mmt2(e):
            ins = None
            for sp in range(8):
                ins = e.matmul(ps_row[0:1, 512 * sp:512 * (sp + 1)], ones32f[:],
                               tmp[:, 512 * sp:512 * (sp + 1)], start=True, stop=True)
            return ins
        p.emit("T", mmt2, rd=[tmp, ones32f], wr=[ps_row])
        p.emit("V", lambda e: e.tensor_copy(rowp[0:1, D:2 * D], ps_row[0:1, :]), rd=[ps_row],
               wr=[rowp.box(fs=(D, 2 * D))])
        p.barrier()
        p.emit("V", lambda e: e.tensor_mul(tmp[:], bt_f[:], gb_f[:]), rd=[bt_f, gb_f], wr=[tmp])

        def mmt3(e):
            ins = None
            for sp in range(8):
                ins = e.matmul(ps_row[0:1, 512 * sp:512 * (sp + 1)], ones32f[:],
                               tmp[:, 512 * sp:512 * (sp + 1)], start=True, stop=True)
            return ins
        p.emit("T", mmt3, rd=[tmp, ones32f], wr=[ps_row])
        p.emit("V", lambda e: e.tensor_copy(rowp[0:1, 2 * D:3 * D], ps_row[0:1, :]), rd=[ps_row],
               wr=[rowp.box(fs=(2 * D, 3 * D))])
        p.emit("V", lambda e: e.scalar_tensor_tensor(
            rowp[0:1, 2 * D:3 * D], rowp[0:1, D:2 * D], 2.0, rowp[0:1, 2 * D:3 * D], ALU.mult, ALU.add),
            rd=[rowp.box(fs=(D, 2 * D)), rowp.box(fs=(2 * D, 3 * D))], wr=[rowp.box(fs=(2 * D, 3 * D))])
        p.emit("V", lambda e: e.tensor_add(rowp[0:1, 2 * D:3 * D], rowp[0:1, 2 * D:3 * D], rowp[0:1, 0:D]),
               rd=[rowp.box(fs=(0, D)), rowp.box(fs=(2 * D, 3 * D))], wr=[rowp.box(fs=(2 * D, 3 * D))])
        p.emit("AD", lambda e: e.dma_start(out=tmp[0:1, 0:D], in_=mag.ap()),
               rd=[mag], wr=[tmp.box(ps=(0, 1), fs=(0, D))])
        p.emit("A", lambda e: e.activation(rowp[0:1, 2 * D:3 * D], rowp[0:1, 2 * D:3 * D], AF.Ln),
               rd=[rowp.box(fs=(2 * D, 3 * D))], wr=[rowp.box(fs=(2 * D, 3 * D))])
        p.emit("A", lambda e: e.activation(rowp[0:1, 2 * D:3 * D], rowp[0:1, 2 * D:3 * D], AF.Exp, scale=-0.5),
               rd=[rowp.box(fs=(2 * D, 3 * D))], wr=[rowp.box(fs=(2 * D, 3 * D))])
        p.emit("V", lambda e: e.tensor_mul(c_row[:], rowp[0:1, 2 * D:3 * D], tmp[0:1, 0:D]),
               rd=[rowp.box(fs=(2 * D, 3 * D)), tmp.box(ps=(0, 1), fs=(0, D))], wr=[c_row])

    with p.phase("docrep"):
        ones_f = p.sbuf("do_onf", [1, 128], F32)
        ps_cr = [p.psum("do_pscr", [128, 512], F32) for _ in range(2)]
        p.emit("V", lambda e: e.memset(ones_f[:], 1.0), wr=[ones_f])
        for sp in range(8):
            pslot = ps_cr[sp % 2]
            p.emit("T", lambda e, sp=sp, pslot=pslot: e.matmul(
                pslot[:], ones_f[:], c_row[0:1, 512 * sp:512 * (sp + 1)], start=True, stop=True),
                rd=[ones_f, c_row], wr=[pslot])
            p.emit("V", lambda e, sp=sp, pslot=pslot: e.tensor_copy(
                c_rep[:, 512 * sp:512 * (sp + 1)], pslot[:]),
                rd=[pslot], wr=[c_rep.box(fs=(512 * sp, 512 * (sp + 1)))])

    wt_v = wt.ap().rearrange("(c p) o -> p c o", p=128)
    xT_v = xT_h.ap().rearrange("(c d) t -> d c t", d=128)
    with p.phase("dobase"):
        wq_sb = p.sbuf("do_wq", [128, ND * 1024], BF16)
        xtt = [p.sbuf("do_xtt", [128, ND * 128], BF16) for _ in range(2)]
        ps_o = [p.psum("do_pso", [128, 1024], F32) for _ in range(2)]
        otile = [p.sbuf("do_ot", [128, 1024], BF16) for _ in range(2)]
        for q in range(4):
            p.emit("GD", lambda e, q=q: e.dma_start(out=wq_sb[:], in_=wt_v[:, :, 1024 * q:1024 * (q + 1)]),
                   rd=[wt], wr=[wq_sb])
            for t in range(NT):
                xt, ps, ot = xtt[t % 2], ps_o[t % 2], otile[t % 2]
                p.emit("S", lambda e, xt=xt, t=t: e.dma_start(
                    out=xt[:], in_=xT_v[:, :, 128 * t:128 * (t + 1)]),
                    rd=[xT_h], wr=[xt])

                def mm(e, xt=xt, ps=ps, q=q, t=t):
                    ins = None
                    for i in range(ND):
                        for sp in range(2):
                            ins = e.matmul(ps[:, 512 * sp:512 * (sp + 1)],
                                           xt[:, 128 * i:128 * (i + 1)],
                                           wq_sb[:, 1024 * i + 512 * sp:1024 * i + 512 * (sp + 1)],
                                           start=(i == 0), stop=False)
                    for sp in range(2):
                        ins = e.matmul(ps[:, 512 * sp:512 * (sp + 1)],
                                       xa_sb[0:32, 128 * t:128 * (t + 1)],
                                       bt_bf[:, 1024 * q + 512 * sp:1024 * q + 512 * (sp + 1)],
                                       start=False, stop=True)
                    return ins
                p.emit("T", mm, rd=[xt, wq_sb, xa_sb.box(fs=(128 * t, 128 * (t + 1))),
                                    bt_bf.box(fs=(1024 * q, 1024 * (q + 1)))], wr=[ps])
                p.emit("V", lambda e, ps=ps, ot=ot, q=q: e.tensor_mul(
                    ot[:], ps[:], c_rep[:, 1024 * q:1024 * (q + 1)]),
                    rd=[ps, c_rep.box(fs=(1024 * q, 1024 * (q + 1)))], wr=[ot])
                p.emit("GD", lambda e, ot=ot, q=q, t=t: e.dma_start(
                    out=sub_h[128 * t:128 * (t + 1), 1024 * q:1024 * (q + 1)], in_=ot[:]),
                    rd=[ot], wr=[sub_h.box(ps=(128 * t, 128 * (t + 1)), fs=(1024 * q, 1024 * (q + 1)))])




def emit_proj_tm(p, xT_h, w_sb, n_out, identb, out_h=None, outT_h=None, out_col0=0,
                 outT_row0=0, tag="pj"):
    xT_v = xT_h.ap().rearrange("(c d) t -> d c t", d=128)
    nsp = ceil_div(n_out, 512)
    nch = n_out // 128
    with p.phase(tag):
        xtt = [p.sbuf(tag + "xtt", [128, ND * 128], BF16) for _ in range(2)]
        nps = 1 if n_out > 1024 else 2
        ps = [p.psum(tag + "ps", [128, n_out], F32) for _ in range(nps)]
        ps = [ps[i % nps] for i in range(2)]
        tm = [p.sbuf(tag + "tm", [128, n_out], BF16) for _ in range(2)]
        if outT_h is not None:
            psT = [p.psum(tag + "psT", [128, 128 * min(8, nch)], F32) for _ in range(2)]
            stg = [p.sbuf(tag + "stg", [128, 128 * min(8, nch)], BF16) for _ in range(2)]
            outT_v = outT_h.ap().rearrange("(c d) t -> d c t", d=128)
        for t in range(NT):
            xt, pst, tmt = xtt[t % 2], ps[t % 2], tm[t % 2]
            p.emit("S", lambda e, xt=xt, t=t: e.dma_start(
                out=xt[:], in_=xT_v[:, :, 128 * t:128 * (t + 1)]),
                rd=[xT_h], wr=[xt])

            def mm(e, xt=xt, pst=pst):
                ins = None
                for i in range(ND):
                    for sp in range(nsp):
                        w = min(512, n_out - 512 * sp)
                        ins = e.matmul(pst[:, 512 * sp:512 * sp + w],
                                       xt[:, 128 * i:128 * (i + 1)],
                                       w_sb[:, n_out * i + 512 * sp:n_out * i + 512 * sp + w],
                                       start=(i == 0), stop=(i == ND - 1))
                return ins
            p.emit("T", mm, rd=[xt, w_sb], wr=[pst])
            p.emit("A", lambda e, pst=pst, tmt=tmt: e.copy(tmt[:], pst[:]), rd=[pst], wr=[tmt])
            if out_h is not None:
                p.emit("GD", lambda e, tmt=tmt, t=t: e.dma_start(
                    out=out_h[128 * t:128 * (t + 1), out_col0:out_col0 + n_out], in_=tmt[:]),
                    rd=[tmt], wr=[out_h.box(ps=(128 * t, 128 * (t + 1)),
                                            fs=(out_col0, out_col0 + n_out))])
            if outT_h is not None:
                for g0 in range(0, nch, 8):
                    gn = min(8, nch - g0)
                    pt = psT[(t + g0 // 8) % 2]
                    sg = stg[(t + g0 // 8) % 2]

                    def tr(e, tmt=tmt, pt=pt, g0=g0, gn=gn):
                        ins = None
                        for u in range(gn):
                            ins = e.matmul(pt[:, 128 * u:128 * (u + 1)],
                                           tmt[:, 128 * (g0 + u):128 * (g0 + u + 1)],
                                           identb[:], start=True, stop=True)
                        return ins
                    p.emit("T", tr, rd=[tmt, identb], wr=[pt])
                    p.emit("V", lambda e, pt=pt, sg=sg, gn=gn: e.tensor_copy(
                        sg[:, 0:128 * gn], pt[:, 0:128 * gn]), rd=[pt], wr=[sg])
                    c0 = (outT_row0 // 128) + g0
                    p.emit("GD", lambda e, sg=sg, c0=c0, gn=gn, t=t: e.dma_start(
                        out=outT_v[:, c0:c0 + gn, 128 * t:128 * (t + 1)],
                        in_=sg[:, 0:128 * gn].rearrange("d (c t) -> d c t", t=128)),
                        rd=[sg], wr=[outT_h.box(ps=(128 * c0, 128 * (c0 + gn)),
                                                fs=(128 * t, 128 * (t + 1)))])


def load_w_panel(p, w_dram, w_sb, n_out, col0=0):
    w_v = w_dram.ap().rearrange("(c p) o -> p c o", p=128)
    p.emit("GD", lambda e: e.dma_start(out=w_sb[:], in_=w_v[:, :, col0:col0 + n_out]),
           rd=[w_dram], wr=[w_sb])


def emit_proj_from_h(p, src_h, w_dram, out_h, epilogue=None, extra=None, tag="oj"):
    w_v = w_dram.ap().rearrange("(c p) o -> p c o", p=128)
    src_v = src_h.ap().rearrange("(c d) t -> d c t", d=128)
    with p.phase(tag):
        wq_sb = p.sbuf(tag + "wq", [128, ND * 1024], BF16)
        xtt = [p.sbuf(tag + "xtt", [128, ND * 128], BF16) for _ in range(2)]
        ps_o = [p.psum(tag + "ps", [128, 1024], F32) for _ in range(2)]
        otile = [p.sbuf(tag + "ot", [128, 1024], BF16) for _ in range(2)]
        for q in range(4):
            p.emit("GD", lambda e, q=q: e.dma_start(out=wq_sb[:], in_=w_v[:, :, 1024 * q:1024 * (q + 1)]),
                   rd=[w_dram], wr=[wq_sb])
            for t in range(NT):
                xt, ps, ot = xtt[t % 2], ps_o[t % 2], otile[t % 2]
                p.emit("S", lambda e, xt=xt, t=t: e.dma_start(
                    out=xt[:], in_=src_v[:, :, 128 * t:128 * (t + 1)]),
                    rd=[src_h], wr=[xt])

                def mm(e, xt=xt, ps=ps):
                    ins = None
                    for i in range(ND):
                        for sp in range(2):
                            ins = e.matmul(ps[:, 512 * sp:512 * (sp + 1)],
                                           xt[:, 128 * i:128 * (i + 1)],
                                           wq_sb[:, 1024 * i + 512 * sp:1024 * i + 512 * (sp + 1)],
                                           start=(i == 0), stop=(i == ND - 1))
                    return ins
                p.emit("T", mm, rd=[xt, wq_sb], wr=[ps])
                if epilogue is None:
                    p.emit("A", lambda e, ps=ps, ot=ot: e.copy(ot[:], ps[:]), rd=[ps], wr=[ot])
                else:
                    epilogue(p, ps, ot, q)
                p.emit("GD", lambda e, ot=ot, q=q, t=t: e.dma_start(
                    out=out_h[128 * t:128 * (t + 1), 1024 * q:1024 * (q + 1)], in_=ot[:]),
                    rd=[ot], wr=[out_h.box(ps=(128 * t, 128 * (t + 1)), fs=(1024 * q, 1024 * (q + 1)))])



def emit_l1_attn(p, x_in, wqt, wkt, wvt, wot, xT_h, qT_h, kT_h, v_h, attnT_h, sub_h, identb):
    emit_build_xT(p, x_in, xT_h, identb)
    with p.phase("l1kvw"):
        wk_sb = p.sbuf("l1wk", [128, ND * 1024], BF16)
        load_w_panel(p, wkt, wk_sb, 1024)
        emit_proj_tm(p, xT_h, wk_sb, 1024, identb, outT_h=kT_h, tag="l1k")
        load_w_panel(p, wvt, wk_sb, 1024)
        emit_proj_tm(p, xT_h, wk_sb, 1024, identb, out_h=v_h, tag="l1v")
    with p.phase("l1qw"):
        wq_sb = p.sbuf("l1wq", [128, ND * 2048], BF16)
        for half in range(2):
            load_w_panel(p, wqt, wq_sb, 2048, col0=2048 * half)
            emit_proj_tm(p, xT_h, wq_sb, 2048, identb, outT_h=qT_h,
                         outT_row0=2048 * half, tag=f"l1q{half}")
    with p.phase("l1at"):
        SC = 1.0 / np.sqrt(128.0)
        qh = [p.sbuf("l1qh", [128, S], BF16) for _ in range(2)]
        kh = [p.sbuf("l1kh", [128, S], BF16) for _ in range(2)]
        vh = [p.sbuf("l1vh", [128, 16 * 128], BF16) for _ in range(2)]
        P_sb = [p.sbuf("l1P", [128, S], BF16) for _ in range(2)]
        PT_sb = [p.sbuf("l1PT", [128, S], BF16) for _ in range(2)]
        stage = [p.sbuf("l1st", [128, S], BF16) for _ in range(2)]
        psR = [p.psum("l1ps", [128, S], F32) for _ in range(2)]
        cols = p.sbuf("l1c", [128, 8], F32)
        v_vv = v_h.ap().rearrange("(c p) o -> p c o", p=128)
        it = 0
        items = [(hk, g, qt) for hk in range(8) for g in range(4) for qt in range(NT)]
        pend = None
        for (hk, g, qt) in items:
            h = hk * 4 + g
            par = it % 2
            R = psR[par]
            if qt == 0:
                p.emit("S", lambda e, q=qh[h % 2], h=h: e.dma_start(
                    out=q[:], in_=qT_h[128 * h:128 * (h + 1), :]),
                    rd=[qT_h.box(ps=(128 * h, 128 * (h + 1)))], wr=[qh[h % 2]])
                if g == 0:
                    p.emit("S", lambda e, k=kh[hk % 2], hk=hk: e.dma_start(
                        out=k[:], in_=kT_h[128 * hk:128 * (hk + 1), :]),
                        rd=[kT_h.box(ps=(128 * hk, 128 * (hk + 1)))], wr=[kh[hk % 2]])
                    p.emit("S", lambda e, v=vh[hk % 2], hk=hk: e.dma_start(
                        out=v[:], in_=v_vv[:, :, 128 * hk:128 * (hk + 1)]),
                        rd=[v_h], wr=[vh[hk % 2]])
            q_sb = qh[h % 2]
            k_sb = kh[hk % 2]
            def mmqk(e, R=R, q_sb=q_sb, k_sb=k_sb, qt=qt):
                ins = None
                for sp in range(4):
                    ins = e.matmul(R[:, 512 * sp:512 * (sp + 1)], q_sb[:, 128 * qt:128 * (qt + 1)],
                                   k_sb[:, 512 * sp:512 * (sp + 1)], start=True, stop=True)
                return ins
            p.emit("T", mmqk, rd=[q_sb.box(fs=(128 * qt, 128 * (qt + 1))), k_sb], wr=[R])
            p.emit("V", lambda e, R=R: e.reduce_max(cols[:, 0:1], R[:], axis=AX.X, negate=True),
                   rd=[R], wr=[cols.box(fs=(0, 1))])
            p.emit("V", lambda e: e.tensor_scalar_mul(cols[:, 1:2], cols[:, 0:1], SC),
                   rd=[cols.box(fs=(0, 1))], wr=[cols.box(fs=(1, 2))])
            Pp = P_sb[par]
            p.emit("A", lambda e, R=R, Pp=Pp: e.activation(
                Pp[:], R[:], AF.Exp, bias=cols[:, 1:2], scale=SC, accum_out=cols[:, 2:3]),
                rd=[R, cols.box(fs=(1, 2))], wr=[Pp, cols.box(fs=(2, 3))])
            p.emit("V", lambda e: e.reciprocal(cols[:, 3:4], cols[:, 2:3]),
                   rd=[cols.box(fs=(2, 3))], wr=[cols.box(fs=(3, 4))])
            p.emit("V", lambda e, Pp=Pp: e.tensor_scalar_mul(Pp[:], Pp[:], cols[:, 3:4]),
                   rd=[Pp, cols.box(fs=(3, 4))], wr=[Pp])
            def mmpt(e, R=R, Pp=Pp):
                ins = None
                for j in range(16):
                    ins = e.matmul(R[:, 128 * j:128 * (j + 1)], Pp[:, 128 * j:128 * (j + 1)],
                                   identb[:], start=True, stop=True)
                return ins
            p.emit("T", mmpt, rd=[Pp, identb], wr=[R])
            PTp = PT_sb[par]
            p.emit("A", lambda e, R=R, PTp=PTp: e.copy(PTp[:], R[:]), rd=[R], wr=[PTp])
            v_sb = vh[hk % 2]

            def mmav(e, R=R, PTp=PTp, v_sb=v_sb):
                ins = None
                for j in range(16):
                    ins = e.matmul(R[:, 0:128], v_sb[:, 128 * j:128 * (j + 1)],
                                   PTp[:, 128 * j:128 * (j + 1)], start=(j == 0), stop=(j == 15))
                return ins
            p.emit("T", mmav, rd=[PTp, v_sb], wr=[R.box(fs=(0, 512))])
            st = stage[h % 2]
            p.emit("A", lambda e, R=R, st=st, qt=qt: e.copy(st[:, 128 * qt:128 * (qt + 1)], R[:, 0:128]),
                   rd=[R.box(fs=(0, 512))], wr=[st.box(fs=(128 * qt, 128 * (qt + 1)))])
            if qt == NT - 1:
                p.emit("GD", lambda e, st=st, h=h: e.dma_start(
                    out=attnT_h[128 * h:128 * (h + 1), :], in_=st[:]),
                    rd=[st], wr=[attnT_h.box(ps=(128 * h, 128 * (h + 1)))])
            it += 1
    emit_proj_from_h(p, attnT_h, wot, sub_h, tag="l1o")


def build_l1attn(out_kind="ExternalOutput", debug=False):
    nc = bass.Bass("TRN2", target_bir_lowering=False, debug=False)
    with ExitStack() as ctx:
        p = Prog(nc, ctx)
        x_in = p.io("x_in", [S, D], F32, "ExternalInput")
        wqt = p.io("wqt", [D, D], F32, "ExternalInput")
        wkt = p.io("wkt", [D, 1024], F32, "ExternalInput")
        wvt = p.io("wvt", [D, 1024], F32, "ExternalInput")
        wot = p.io("wot", [D, D], F32, "ExternalInput")
        bo_rep = p.io("bo_rep", [128, D], F32, "ExternalInput")
        identd = p.io("identd", [128, 128], F32, "ExternalInput")
        x_out = p.io("x_out", [S, D], F32, out_kind)
        xT_h = p.dram("xT_h", [D, S], BF16)
        qT_h = p.dram("qT_h", [D, S], BF16)
        kT_h = p.dram("kT_h", [1024, S], BF16)
        v_h = p.dram("v_h", [S, 1024], BF16)
        attnT_h = p.dram("attnT_h", [D, S], BF16)
        sub_h = p.dram("sub_h", [S, D], BF16)
        identb = p.sbuf("identb", [128, 128], BF16)
        bo_sb = p.sbuf("bo_sb", [128, D], F32)
        p.emit("GD", lambda e: e.dma_start(out=identb[:], in_=identd.ap()), rd=[identd], wr=[identb])
        p.emit("S", lambda e: e.dma_start(out=bo_sb[:], in_=bo_rep.ap()), rd=[bo_rep], wr=[bo_sb])
        emit_l1_attn(p, x_in, wqt, wkt, wvt, wot, xT_h, qT_h, kT_h, v_h, attnT_h, sub_h, identb)
        if debug:
            p.tap(qT_h, "d_qT"); p.tap(kT_h, "d_kT"); p.tap(v_h, "d_v")
            p.tap(attnT_h, "d_attnT"); p.tap(sub_h, "d_sub")
        emit_ln_residual(p, sub_h, x_in, x_out, extra_row=bo_sb)
        p.finalize()
    return nc


def _l1attn_maps(x_full, inputs):
    wqt = np.ascontiguousarray(inputs["L1_attn_Wq"].T)
    wkt = np.ascontiguousarray(inputs["L1_attn_Wk"].T)
    wvt = np.ascontiguousarray(inputs["L1_attn_Wv"].T)
    wot = np.ascontiguousarray(inputs["L1_attn_Wo"].T)
    bo_rep = np.ascontiguousarray(np.broadcast_to(inputs["L1_attn_bo"], (128, D))).astype(np.float32)
    identd = np.eye(128, dtype=np.float32)
    return [{"x_in": np.ascontiguousarray(x_full[c]), "wqt": wqt, "wkt": wkt, "wvt": wvt,
             "wot": wot, "bo_rep": bo_rep, "identd": identd} for c in range(NCORES)]


def run_l1attn(x_full, inputs, debug=False):
    nc = build_l1attn(debug=debug)
    res = run_bass_kernel_spmd(nc, _l1attn_maps(x_full, inputs), core_ids=list(range(NCORES)))
    if debug:
        return res
    return np.stack([r["x_out"] for r in res.results], 0)




INTER = 11008
NI = INTER // 128


def emit_dequant(p, qw, scales, zeros, shamt, w_h, K, N, tag):
    G = K // 128
    NP = N // 8
    zrow_h = p.dram(tag + "zrh", [1, N], F32)
    with p.phase(tag):
        QR = p.sbuf(tag + "qr", [128, N], I32)
        S_rep = p.sbuf(tag + "s", [128, N], F32)
        Z_rep = p.sbuf(tag + "z", [128, N], F32)
        wbf = p.sbuf(tag + "w", [128, N], BF16)
        zq = p.sbuf(tag + "zq", [1, NP], I32)
        srow = p.sbuf(tag + "sr", [1, N], F16)
        zrow = Z_rep
        sh_sb = p.sbuf(tag + "sh", [128, 1], I32)
        p.emit("AD", lambda e: e.dma_start(out=sh_sb[:], in_=shamt.ap()), rd=[shamt], wr=[sh_sb])
        for kt in range(G):
            if "0" in XBAR:
                p.barrier()
            g = kt
            p.emit("GD", lambda e, kt=kt: e.dma_start(
                out=QR[:], in_=qw[16 * kt:16 * (kt + 1), :].unsqueeze(1).broadcast_to([16, 8, N])),
                rd=[qw.box(ps=(16 * kt, 16 * (kt + 1)))], wr=[QR])
            p.emit("AD", lambda e, g=g: e.dma_start(out=srow[:], in_=scales[g:g + 1, :]),
                   rd=[scales.box(ps=(g, g + 1))], wr=[srow])
            p.emit("AD", lambda e, g=g: e.dma_start(out=zq[:], in_=zeros[g:g + 1, :]),
                   rd=[zeros.box(ps=(g, g + 1))], wr=[zq])
            zi_v = zrow[0:1, :].bitcast(I32).rearrange("o (a b) -> o a b", b=8)
            for r in range(8):
                p.emit("V", lambda e, r=r: e.tensor_scalar(
                    zi_v[:, :, r], zq[:], 4 * r, 15, ALU.logical_shift_right, ALU.bitwise_and),
                    rd=[zq], wr=[zrow.box(ps=(0, 1))])
            p.emit("V", lambda e: e.tensor_scalar(
                zrow[0:1, :], zrow[0:1, :].bitcast(I32), 1.0, None, ALU.add),
                rd=[zrow.box(ps=(0, 1))], wr=[zrow.box(ps=(0, 1))])
            p.emit("V", lambda e: e.tensor_mul(zrow[0:1, :], zrow[0:1, :], srow[:]),
                   rd=[zrow.box(ps=(0, 1)), srow], wr=[zrow.box(ps=(0, 1))])
            p.emit("GD", lambda e, g=g: e.dma_start(
                out=S_rep[:], in_=scales[g:g + 1, :].broadcast_to([128, N])),
                rd=[scales.box(ps=(g, g + 1))], wr=[S_rep])
            p.emit("GD", lambda e: e.dma_start(out=zrow_h.ap(), in_=zrow[0:1, :]),
                   rd=[zrow.box(ps=(0, 1))], wr=[zrow_h])
            p.emit("GD", lambda e: e.dma_start(
                out=Z_rep[:], in_=zrow_h.ap().broadcast_to([128, N])),
                rd=[zrow_h], wr=[Z_rep])
            p.emit("V", lambda e: e.tensor_scalar(
                QR[:], QR[:], sh_sb[:], 15, ALU.logical_shift_right, ALU.bitwise_and),
                rd=[QR, sh_sb], wr=[QR])
            p.emit("V", lambda e: e.tensor_mul(S_rep[:], QR[:], S_rep[:]), rd=[QR, S_rep], wr=[S_rep])
            p.emit("V", lambda e: e.tensor_sub(wbf[:], S_rep[:], Z_rep[:]), rd=[S_rep, Z_rep], wr=[wbf])
            p.emit("GD", lambda e, kt=kt: e.dma_start(
                out=w_h[128 * kt:128 * (kt + 1), :], in_=wbf[:]),
                rd=[wbf], wr=[w_h.box(ps=(128 * kt, 128 * (kt + 1)))])


def emit_gptq_ffn(p, x_in, gate_h, up_h, down_h, xT_h, interT_h, sub_h, identb):
    emit_build_xT(p, x_in, xT_h, identb)
    xT_v = xT_h.ap().rearrange("(c d) t -> d c t", d=128)
    with p.phase("f1"):
        wg_sb = [p.sbuf("f1wg", [128, ND * 512], BF16) for _ in range(2)]
        wu_sb = [p.sbuf("f1wu", [128, ND * 512], BF16) for _ in range(2)]
        xtt = [p.sbuf("f1xt", [128, ND * 128], BF16) for _ in range(2)]
        psg = [p.psum("f1pg", [128, 512], F32) for _ in range(2)]
        psu = [p.psum("f1pu", [128, 512], F32) for _ in range(2)]
        psT = [p.psum("f1pT", [128, 512], F32) for _ in range(2)]
        sil = [p.sbuf("f1sl", [128, 512], BF16) for _ in range(2)]
        ich = [p.sbuf("f1ic", [128, 512], BF16) for _ in range(2)]
        stg = [p.sbuf("f1st", [128, 512], BF16) for _ in range(2)]
        g_v = gate_h.ap().rearrange("(c p) o -> p c o", p=128)
        u_v = up_h.ap().rearrange("(c p) o -> p c o", p=128)
        iT_v = interT_h.ap().rearrange("c i t -> i c t")
        NSP = INTER // 512
        spans = [(sp * 512, min(512, INTER - sp * 512)) for sp in range((INTER + 511) // 512)]
        for spi, (c0, w) in enumerate(spans):
            if "1" in XBAR:
                p.barrier()
            wg = wg_sb[spi % 2]
            wu = wu_sb[spi % 2]
            p.emit("GD", lambda e, wg=wg, c0=c0, w=w: e.dma_start(
                out=wg[:, 0:ND * w].rearrange("p (c o) -> p c o", c=ND), in_=g_v[:, :, c0:c0 + w]),
                rd=[gate_h], wr=[wg])
            p.emit("GD", lambda e, wu=wu, c0=c0, w=w: e.dma_start(
                out=wu[:, 0:ND * w].rearrange("p (c o) -> p c o", c=ND), in_=u_v[:, :, c0:c0 + w]),
                rd=[up_h], wr=[wu])
            for t in range(NT):
                xt = xtt[t % 2]
                pg, pu, pt = psg[t % 2], psu[t % 2], psT[t % 2]
                sl, ic, sg = sil[t % 2], ich[t % 2], stg[t % 2]
                if spi == 0 or True:
                    p.emit("S", lambda e, xt=xt, t=t: e.dma_start(
                        out=xt[:], in_=xT_v[:, :, 128 * t:128 * (t + 1)]),
                        rd=[xT_h], wr=[xt])

                def mm(e, xt=xt, pg=pg, pu=pu, wg=wg, wu=wu, w=w):
                    ins = None
                    for i in range(ND):
                        ins = e.matmul(pg[:, 0:w], xt[:, 128 * i:128 * (i + 1)],
                                       wg[:, w * i:w * (i + 1)], start=(i == 0), stop=(i == ND - 1))
                    for i in range(ND):
                        ins = e.matmul(pu[:, 0:w], xt[:, 128 * i:128 * (i + 1)],
                                       wu[:, w * i:w * (i + 1)], start=(i == 0), stop=(i == ND - 1))
                    return ins
                p.emit("T", mm, rd=[xt, wg, wu], wr=[pg, pu])
                p.emit("A", lambda e, pg=pg, sl=sl, w=w: e.activation(sl[:, 0:w], pg[:, 0:w], AF.Silu),
                       rd=[pg], wr=[sl])
                p.emit("V", lambda e, pu=pu, sl=sl, ic=ic, w=w: e.tensor_mul(
                    ic[:, 0:w], pu[:, 0:w], sl[:, 0:w]), rd=[pu, sl], wr=[ic])

                nch = w // 128

                def tr(e, ic=ic, pt=pt, nch=nch):
                    ins = None
                    for u in range(nch):
                        ins = e.matmul(pt[:, 128 * u:128 * (u + 1)], ic[:, 128 * u:128 * (u + 1)],
                                       identb[:], start=True, stop=True)
                    return ins
                p.emit("T", tr, rd=[ic, identb], wr=[pt])
                p.emit("V", lambda e, pt=pt, sg=sg, nch=nch: e.tensor_copy(
                    sg[:, 0:128 * nch], pt[:, 0:128 * nch]), rd=[pt], wr=[sg])
                ch0 = c0 // 128
                p.emit("GD", lambda e, sg=sg, ch0=ch0, nch=nch, t=t: e.dma_start(
                    out=iT_v[:, ch0:ch0 + nch, 128 * t:128 * (t + 1)],
                    in_=sg[:, 0:128 * nch].rearrange("i (c t) -> i c t", t=128)),
                    rd=[sg], wr=[interT_h.box(ps=(ch0, ch0 + nch))])
    emit_f2_only(p, down_h, interT_h, sub_h)


def emit_f2_only(p, down_h, interT_h, sub_h):
    GRP = 6
    ngrp = (NI + GRP - 1) // GRP
    with p.phase("f2"):
        accb = p.sbuf("f2ac", [128, 4 * D], F32)
        wdp = [p.sbuf("f2wd", [128, D], BF16) for _ in range(2 * GRP)]
        itp = [p.sbuf("f2it", [128, 512], BF16) for _ in range(2 * GRP)]
        pso = p.psum("f2ps", [128, D], F32)
        of16 = [p.sbuf("f2o", [128, D], F16) for _ in range(2)]
        for sup in range(4):
            for gi in range(ngrp):
                if "2" in XBAR:
                    p.barrier()
                ids = list(range(gi * GRP, min(NI, (gi + 1) * GRP)))
                for u, i in enumerate(ids):
                    b = (gi % 2) * GRP + u
                    p.emit("S", lambda e, b=b, i=i: e.dma_start(
                        out=wdp[b][:], in_=down_h[128 * i:128 * (i + 1), :]),
                        rd=[down_h.box(ps=(128 * i, 128 * (i + 1)))], wr=[wdp[b]])
                    p.emit("GD", lambda e, b=b, i=i, sup=sup: e.dma_start(
                        out=itp[b][:], in_=interT_h.ap()[i:i + 1, :, 512 * sup:512 * (sup + 1)]),
                        rd=[interT_h.box(ps=(i, i + 1))], wr=[itp[b]])
                if "3" in XBAR:
                    p.barrier()
                for tl in range(4):
                    def mm(e, gi=gi, n_ids=len(ids), tl=tl):
                        ins = None
                        for u in range(n_ids):
                            b = (gi % 2) * GRP + u
                            for sp in range(8):
                                ins = e.matmul(pso[:, 512 * sp:512 * (sp + 1)],
                                               itp[b][:, 128 * tl:128 * (tl + 1)],
                                               wdp[b][:, 512 * sp:512 * (sp + 1)],
                                               start=(u == 0), stop=(u == n_ids - 1))
                        return ins
                    rds = [itp[(gi % 2) * GRP + u] for u in range(len(ids))] + \
                          [wdp[(gi % 2) * GRP + u] for u in range(len(ids))]
                    p.emit("T", mm, rd=rds, wr=[pso])
                    if "4" in XBAR:
                        p.barrier()
                    if gi == 0:
                        p.emit("V", lambda e, tl=tl: e.tensor_copy(
                            accb[:, D * tl:D * (tl + 1)], pso[:]), rd=[pso],
                            wr=[accb.box(fs=(D * tl, D * (tl + 1)))])
                    else:
                        p.emit("V", lambda e, tl=tl: e.tensor_add(
                            accb[:, D * tl:D * (tl + 1)], accb[:, D * tl:D * (tl + 1)], pso[:]),
                            rd=[pso, accb.box(fs=(D * tl, D * (tl + 1)))],
                            wr=[accb.box(fs=(D * tl, D * (tl + 1)))])
            for tl in range(4):
                o = of16[tl % 2]
                p.emit("V", lambda e, tl=tl, o=o: e.tensor_copy(o[:], accb[:, D * tl:D * (tl + 1)]),
                       rd=[accb.box(fs=(D * tl, D * (tl + 1)))], wr=[o])
                t = sup * 4 + tl
                p.emit("GD", lambda e, o=o, t=t: e.dma_start(
                    out=sub_h[128 * t:128 * (t + 1), :], in_=o[:]),
                    rd=[o], wr=[sub_h.box(ps=(128 * t, 128 * (t + 1)))])


def build_l0ffn_f2only():
    nc = bass.Bass("TRN2", target_bir_lowering=False, debug=False)
    with ExitStack() as ctx:
        p = Prog(nc, ctx)
        x_in = p.io("x_in", [S, D], F32, "ExternalInput")
        dq = p.io("dq", [INTER // 8, D], I32, "ExternalInput")
        ds = p.io("ds", [86, D], F16, "ExternalInput")
        dz = p.io("dz", [86, D // 8], I32, "ExternalInput")
        shamt = p.io("shamt", [128, 1], I32, "ExternalInput")
        interT_in = p.io("interT_in", [NI, 128, S], F32, "ExternalInput")
        x_out = p.io("x_out", [S, D], F32, "ExternalOutput")
        down_h = p.dram("down_h", [INTER, D], BF16)
        interT_h = p.dram("interT_h", [NI, 128, S], BF16)
        sub_h = p.dram("sub_h", [S, D], F16)
        p.tap_in = None
        for i in range(NI):
            p.emit("GD", lambda e, i=i: e.dma_start(
                out=interT_h.ap()[i:i + 1], in_=interT_in.ap()[i:i + 1]),
                rd=[interT_in.box(ps=(i, i + 1))], wr=[interT_h.box(ps=(i, i + 1))])
        emit_dequant(p, dq, ds, dz, shamt, down_h, INTER, D, "dd")
        xT_h = p.dram("xT_h", [D, S], BF16)
        class _NS: pass
        emit_f2_only(p, down_h, interT_h, sub_h)
        p.tap(sub_h, "d_sub")
        emit_ln_residual(p, sub_h, x_in, x_out, sub_dtype=F16)
        p.finalize()
    return nc


def run_l0ffn_f2only(x_full, inputs, interT):
    nc = build_l0ffn_f2only()
    shamt = (4 * (np.arange(128) % 8)).astype(np.int32).reshape(128, 1)
    maps = [{"x_in": np.ascontiguousarray(x_full[c]), "dq": inputs["L0_ffn_down_qweight"],
             "ds": inputs["L0_ffn_down_scales"], "dz": inputs["L0_ffn_down_zeros"],
             "shamt": shamt, "interT_in": interT[c]} for c in range(NCORES)]
    return run_bass_kernel_spmd(nc, maps, core_ids=list(range(NCORES)))


def build_l0ffn(out_kind="ExternalOutput", debug=False):
    nc = bass.Bass("TRN2", target_bir_lowering=False, debug=False)
    with ExitStack() as ctx:
        p = Prog(nc, ctx)
        x_in = p.io("x_in", [S, D], F32, "ExternalInput")
        gq = p.io("gq", [D // 8, INTER], I32, "ExternalInput")
        gs = p.io("gs", [32, INTER], F16, "ExternalInput")
        gz = p.io("gz", [32, INTER // 8], I32, "ExternalInput")
        uq = p.io("uq", [D // 8, INTER], I32, "ExternalInput")
        us = p.io("us", [32, INTER], F16, "ExternalInput")
        uz = p.io("uz", [32, INTER // 8], I32, "ExternalInput")
        dq = p.io("dq", [INTER // 8, D], I32, "ExternalInput")
        ds = p.io("ds", [86, D], F16, "ExternalInput")
        dz = p.io("dz", [86, D // 8], I32, "ExternalInput")
        shamt = p.io("shamt", [128, 1], I32, "ExternalInput")
        identd = p.io("identd", [128, 128], F32, "ExternalInput")
        x_out = p.io("x_out", [S, D], F32, out_kind)
        xT_h = p.dram("xT_h", [D, S], BF16)
        gate_h = p.dram("gate_h", [D, INTER], BF16)
        up_h = p.dram("up_h", [D, INTER], BF16)
        down_h = p.dram("down_h", [INTER, D], BF16)
        interT_h = p.dram("interT_h", [NI, 128, S], BF16)
        sub_h = p.dram("sub_h", [S, D], F16)
        identb = p.sbuf("identb", [128, 128], BF16)
        p.emit("GD", lambda e: e.dma_start(out=identb[:], in_=identd.ap()), rd=[identd], wr=[identb])
        emit_dequant(p, gq, gs, gz, shamt, gate_h, D, INTER, "dg")
        emit_dequant(p, uq, us, uz, shamt, up_h, D, INTER, "du")
        emit_dequant(p, dq, ds, dz, shamt, down_h, INTER, D, "dd")
        emit_gptq_ffn(p, x_in, gate_h, up_h, down_h, xT_h, interT_h, sub_h, identb)
        if debug == "sub":
            p.tap(sub_h, "d_sub")
        elif debug:
            p.tap(gate_h, "d_gate"); p.tap(down_h, "d_down")
            p.tap(interT_h, "d_interT"); p.tap(sub_h, "d_sub")
        emit_ln_residual(p, sub_h, x_in, x_out, sub_dtype=F16)
        p.finalize()
    return nc


def _l0ffn_maps(x_full, inputs):
    shamt = (4 * (np.arange(128) % 8)).astype(np.int32).reshape(128, 1)
    identd = np.eye(128, dtype=np.float32)
    base = {"gq": inputs["L0_ffn_gate_qweight"], "gs": inputs["L0_ffn_gate_scales"],
            "gz": inputs["L0_ffn_gate_zeros"], "uq": inputs["L0_ffn_up_qweight"],
            "us": inputs["L0_ffn_up_scales"], "uz": inputs["L0_ffn_up_zeros"],
            "dq": inputs["L0_ffn_down_qweight"], "ds": inputs["L0_ffn_down_scales"],
            "dz": inputs["L0_ffn_down_zeros"], "shamt": shamt, "identd": identd}
    return [dict(base, x_in=np.ascontiguousarray(x_full[c])) for c in range(NCORES)]


def run_l0ffn(x_full, inputs, debug=False):
    nc = build_l0ffn(debug=debug)
    res = run_bass_kernel_spmd(nc, _l0ffn_maps(x_full, inputs), core_ids=list(range(NCORES)))
    if debug:
        return res
    return np.stack([r["x_out"] for r in res.results], 0)




QKD_SC = 192.0 ** -0.5
IDX_SC = 128.0 ** -0.5
IW_SC = 16.0 ** -0.5


def emit_rope_tm(p, out_bf, src_f32, cs_sb, t, nh, tmp1, tmp2):
    xe = src_f32.rearrange("p (h f two) -> p h f two", two=2, f=32)[:, :, :, 0]
    xo = src_f32.rearrange("p (h f two) -> p h f two", two=2, f=32)[:, :, :, 1]
    oe = out_bf.rearrange("p (h f two) -> p h f two", two=2, f=32)[:, :, :, 0]
    oo = out_bf.rearrange("p (h f two) -> p h f two", two=2, f=32)[:, :, :, 1]
    cos = cs_sb[:, 64 * t:64 * t + 32].unsqueeze(1).broadcast_to([128, nh, 32])
    sin = cs_sb[:, 64 * t + 32:64 * t + 64].unsqueeze(1).broadcast_to([128, nh, 32])
    n = nh * 32
    t1 = tmp1[:, 0:n].rearrange("p (h f) -> p h f", f=32)
    t2 = tmp2[:, 0:n].rearrange("p (h f) -> p h f", f=32)
    sb = (p._rope_src_boxes if hasattr(p, "_rope_src_boxes") else None)
    p.emit("V", lambda e: e.tensor_tensor(t1, xe, cos, ALU.mult), rd=[p.cur_src, p.cs_buf], wr=[tmp1])
    p.emit("V", lambda e: e.tensor_tensor(t2, xo, sin, ALU.mult), rd=[p.cur_src, p.cs_buf], wr=[tmp2])
    p.emit("V", lambda e: e.tensor_tensor(oe, t1, t2, ALU.subtract), rd=[tmp1, tmp2], wr=[p.cur_dst])
    p.emit("V", lambda e: e.tensor_tensor(t1, xo, cos, ALU.mult), rd=[p.cur_src, p.cs_buf], wr=[tmp1])
    p.emit("V", lambda e: e.tensor_tensor(t2, xe, sin, ALU.mult), rd=[p.cur_src, p.cs_buf], wr=[tmp2])
    p.emit("V", lambda e: e.tensor_tensor(oo, t1, t2, ALU.add), rd=[tmp1, tmp2], wr=[p.cur_dst])


def emit_rms_proj(p, xT_h, w_sb, n_out, nw_rep, outT_h, identb, tag, rms_n=None,
                  pe_slice=None, peT_h=None, cs_sb=None, iw=None):
    if rms_n is None:
        rms_n = n_out
    xT_v = xT_h.ap().rearrange("(c d) t -> d c t", d=128)
    nch = rms_n // 128
    with p.phase(tag):
        xtt = [p.sbuf(tag + "xtt", [128, ND * 128], BF16) for _ in range(2)]
        nps = 1 if n_out > 640 else 2
        ps = [p.psum(tag + "ps", [128, n_out], F32) for _ in range(nps)]
        ps = [ps[i % nps] for i in range(2)]
        psT = p.psum(tag + "psT", [128, 1024], F32)
        qf = [p.sbuf(tag + "qf", [128, n_out], F32) for _ in range(2)]
        dump = p.sbuf(tag + "dm", [128, n_out], F32)
        tm = [p.sbuf(tag + "tm", [128, rms_n], BF16) for _ in range(2)]
        stg = [p.sbuf(tag + "st", [128, 1024], BF16) for _ in range(2)]
        cols = p.sbuf(tag + "c", [128, 4], F32)
        epsc = p.sbuf(tag + "e", [128, 1], F32)
        p.emit("V", lambda e: e.memset(epsc[:], 1e-5), wr=[epsc])
        if pe_slice is not None:
            c0, nh = pe_slice
            peb = [p.sbuf(tag + "pe", [128, 64 * nh], BF16) for _ in range(2)]
            rt1 = p.sbuf(tag + "r1", [128, 32 * nh], F32)
            rt2 = p.sbuf(tag + "r2", [128, 32 * nh], F32)
            peT_v = peT_h.ap().rearrange("(c d) t -> d c t", d=128) if nh >= 2 else None
        if iw is not None:
            iww_sb, iw_sb, iwsc = iw
            ps_iw = p.psum(tag + "pi", [128, 16], F32)
        outT_v = outT_h.ap().rearrange("(c d) t -> d c t", d=128)
        nsp = ceil_div(n_out, 512)
        for t in range(NT):
            xt, pst, qft, tmt = xtt[t % 2], ps[t % 2], qf[t % 2], tm[t % 2]
            p.emit("S", lambda e, xt=xt, t=t: e.dma_start(
                out=xt[:], in_=xT_v[:, :, 128 * t:128 * (t + 1)]),
                rd=[xT_h], wr=[xt])

            def mm(e, xt=xt, pst=pst):
                ins = None
                for i in range(ND):
                    for sp in range(nsp):
                        w = min(512, n_out - 512 * sp)
                        ins = e.matmul(pst[:, 512 * sp:512 * sp + w],
                                       xt[:, 128 * i:128 * (i + 1)],
                                       w_sb[:, n_out * i + 512 * sp:n_out * i + 512 * sp + w],
                                       start=(i == 0), stop=(i == ND - 1))
                return ins
            p.emit("T", mm, rd=[xt, w_sb], wr=[pst])
            if iw is not None:
                def mmi(e, xt=xt):
                    ins = None
                    for i in range(ND):
                        ins = e.matmul(ps_iw[:], xt[:, 128 * i:128 * (i + 1)],
                                       iww_sb[:, 16 * i:16 * (i + 1)],
                                       start=(i == 0), stop=(i == ND - 1))
                    return ins
                p.emit("T", mmi, rd=[xt, iww_sb], wr=[ps_iw])
                p.emit("A", lambda e, t=t: e.activation(
                    iw_sb[:, 16 * t:16 * (t + 1)], ps_iw[:], AF.Identity, scale=iwsc),
                    rd=[ps_iw], wr=[iw_sb.box(fs=(16 * t, 16 * (t + 1)))])
            p.emit("A", lambda e, pst=pst, qft=qft: e.copy(qft[:], pst[:]), rd=[pst], wr=[qft])
            p.emit("V", lambda e, qft=qft: e.scalar_tensor_tensor(
                dump[:, 0:rms_n], qft[:, 0:rms_n], 1.0, qft[:, 0:rms_n], ALU.mult, ALU.mult,
                accum_out=cols[:, 0:1]),
                rd=[qft], wr=[dump, cols.box(fs=(0, 1))])
            p.emit("V", lambda e: e.tensor_scalar_mul(cols[:, 1:2], cols[:, 0:1], 1.0 / rms_n),
                   rd=[cols.box(fs=(0, 1))], wr=[cols.box(fs=(1, 2))])
            p.emit("A", lambda e: e.activation(cols[:, 2:3], cols[:, 1:2], AF.Ln, bias=epsc[:]),
                   rd=[cols.box(fs=(1, 2)), epsc], wr=[cols.box(fs=(2, 3))])
            p.emit("A", lambda e: e.activation(cols[:, 3:4], cols[:, 2:3], AF.Exp, scale=-0.5),
                   rd=[cols.box(fs=(2, 3))], wr=[cols.box(fs=(3, 4))])
            p.emit("V", lambda e, qft=qft, tmt=tmt: e.scalar_tensor_tensor(
                tmt[:], qft[:, 0:rms_n], cols[:, 3:4], nw_rep[:, 0:rms_n], ALU.mult, ALU.mult),
                rd=[qft, cols.box(fs=(3, 4)), nw_rep], wr=[tmt])
            for g0 in range(0, nch, 8):
                gn = min(8, nch - g0)
                sg = stg[(t + g0 // 8) % 2]

                def tr(e, tmt=tmt, g0=g0, gn=gn):
                    ins = None
                    for u in range(gn):
                        ins = e.matmul(psT[:, 128 * u:128 * (u + 1)],
                                       tmt[:, 128 * (g0 + u):128 * (g0 + u + 1)],
                                       identb[:], start=True, stop=True)
                    return ins
                p.emit("T", tr, rd=[tmt, identb], wr=[psT])
                p.emit("V", lambda e, sg=sg, gn=gn: e.tensor_copy(
                    sg[:, 0:128 * gn], psT[:, 0:128 * gn]), rd=[psT], wr=[sg])
                p.emit("GD", lambda e, sg=sg, g0=g0, gn=gn, t=t: e.dma_start(
                    out=outT_v[:, g0:g0 + gn, 128 * t:128 * (t + 1)],
                    in_=sg[:, 0:128 * gn].rearrange("d (c t) -> d c t", t=128)),
                    rd=[sg], wr=[outT_h.box(ps=(128 * g0, 128 * (g0 + gn)),
                                            fs=(128 * t, 128 * (t + 1)))])
            if pe_slice is not None:
                c0, nh = pe_slice
                pet = peb[t % 2]
                p.cur_src = qft
                p.cur_dst = pet
                p.cs_buf = cs_sb
                emit_rope_tm(p, pet[:], qft[:, c0:c0 + 64 * nh], cs_sb, t, nh, rt1, rt2)
                npc = ceil_div(64 * nh, 128)
                def trp(e, pet=pet, npc=npc, nh=nh):
                    ins = None
                    for u in range(npc):
                        w = min(128, 64 * nh - 128 * u)
                        ins = e.matmul(psT[0:w, 512 + 128 * u:512 + 128 * (u + 1)],
                                       pet[:, 128 * u:128 * u + w], identb[:],
                                       start=True, stop=True)
                    return ins
                p.emit("T", trp, rd=[pet, identb], wr=[psT.box(fs=(512, 1024))])
                sg = stg[t % 2]
                wpe = 64 * nh
                p.emit("V", lambda e, sg=sg, npc=npc, wpe=wpe: e.tensor_copy(
                    sg[:, 512:512 + 128 * npc].rearrange("p (c t) -> p c t", t=128)[0:wpe if npc == 1 else 128, :, :] if False else sg[0:128, 512:512 + 128 * npc],
                    psT[0:128, 512:512 + 128 * npc]), rd=[psT.box(fs=(512, 1024))],
                    wr=[sg.box(fs=(512, 1024))])
                if nh == 1:
                    p.emit("GD", lambda e, sg=sg, t=t: e.dma_start(
                        out=peT_h[0:64, 128 * t:128 * (t + 1)], in_=sg[0:64, 512:640]),
                        rd=[sg.box(ps=(0, 64), fs=(512, 640))],
                        wr=[peT_h.box(ps=(0, 64), fs=(128 * t, 128 * (t + 1)))])
                else:
                    p.emit("GD", lambda e, sg=sg, t=t, npc=npc: e.dma_start(
                        out=peT_v[:, 0:npc, 128 * t:128 * (t + 1)],
                        in_=sg[:, 512:512 + 128 * npc].rearrange("d (c t) -> d c t", t=128)),
                        rd=[sg.box(fs=(512, 512 + 128 * npc))],
                        wr=[peT_h.box(fs=(128 * t, 128 * (t + 1)))])


def emit_l0_attn(p, x_in, ins_d, hbm, identb, tri_sb, cs_sb, sub_h):
    xT_h = hbm["xT_h"]
    emit_build_xT(p, x_in, xT_h, identb)
    S_ALL = S

    iw_sb = p.sbuf("l0iw", [128, NT * 16], F32)

    with p.phase("l0qa_w"):
        qa_sb = p.sbuf("l0qa", [128, ND * 1536], BF16)
        load_w_panel(p, ins_d["qa"], qa_sb, 1536)
        nwq = p.sbuf("l0nwq", [128, 1536], F32)
        p.emit("S", lambda e: e.dma_start(out=nwq[:], in_=ins_d["qanw"].ap()), rd=[ins_d["qanw"]], wr=[nwq])
        emit_rms_proj(p, xT_h, qa_sb, 1536, nwq, hbm["qlatT_h"], identb, "l0qa")

    with p.phase("l0kv_w"):
        kva_sb = p.sbuf("l0kva", [128, ND * 576], BF16)
        load_w_panel(p, ins_d["kva"], kva_sb, 576)
        nwk = p.sbuf("l0nwk", [128, 512], F32)
        p.emit("S", lambda e: e.dma_start(out=nwk[:], in_=ins_d["kvnw"].ap()), rd=[ins_d["kvnw"]], wr=[nwk])
        iww_sb = p.sbuf("l0iww", [128, ND * 16], BF16)
        load_w_panel(p, ins_d["idxw"], iww_sb, 16)
        emit_rms_proj(p, xT_h, kva_sb, 576, nwk, hbm["kvnT_h"], identb, "l0kv", rms_n=512,
                      pe_slice=(512, 1), peT_h=hbm["kpeT_h"], cs_sb=cs_sb,
                      iw=(iww_sb, iw_sb, IW_SC))

    with p.phase("l0ik_w"):
        ikw_sb = p.sbuf("l0ikw", [128, ND * 128], BF16)
        load_w_panel(p, ins_d["idxk"], ikw_sb, 128)
        emit_proj_tm(p, xT_h, ikw_sb, 128, identb, outT_h=hbm["ikT_h"], tag="l0ik")
    with p.phase("l0iq_w"):
        iqw_sb = p.sbuf("l0iqw", [128, ND * 1024], BF16)
        for half in range(2):
            load_w_panel(p, ins_d["idxq"], iqw_sb, 1024, col0=1024 * half)
            emit_proj_tm(p, xT_h, iqw_sb, 1024, identb, outT_h=hbm["iqT_h"],
                         outT_row0=1024 * half, tag=f"l0iq{half}")

    with p.phase("l0qpe"):
        qpw_sb = p.sbuf("l0qpw", [128, 12 * 2048], BF16)
        p.emit("GD", lambda e: e.dma_start(
            out=qpw_sb[:], in_=ins_d["qbpe"].ap().rearrange("(c p) o -> p c o", p=128)),
            rd=[ins_d["qbpe"]], wr=[qpw_sb])
        qlT_v = hbm["qlatT_h"].ap().rearrange("(c d) t -> d c t", d=128)
        qpeT_v = hbm["qpeT_h"].ap().rearrange("(c d) t -> d c t", d=128)
        xtt = [p.sbuf("l0qpx", [128, 12 * 128], BF16) for _ in range(2)]
        psq = p.psum("l0qps", [128, 2048], F32)
        psT = p.psum("l0qpt", [128, 1024], F32)
        qf = [p.sbuf("l0qpf", [128, 2048], F32) for _ in range(2)]
        qb = [p.sbuf("l0qpb", [128, 2048], BF16) for _ in range(2)]
        stg = [p.sbuf("l0qpg", [128, 1024], BF16) for _ in range(2)]
        rt1 = p.sbuf("l0qr1", [128, 1024], F32)
        rt2 = p.sbuf("l0qr2", [128, 1024], F32)
        for t in range(NT):
            xt, qft, qbt = xtt[t % 2], qf[t % 2], qb[t % 2]
            p.emit("S", lambda e, xt=xt, t=t: e.dma_start(
                out=xt[:], in_=qlT_v[:, :, 128 * t:128 * (t + 1)]),
                rd=[hbm["qlatT_h"]], wr=[xt])

            def mm(e, xt=xt):
                ins = None
                for i in range(12):
                    for sp in range(4):
                        ins = e.matmul(psq[:, 512 * sp:512 * (sp + 1)],
                                       xt[:, 128 * i:128 * (i + 1)],
                                       qpw_sb[:, 2048 * i + 512 * sp:2048 * i + 512 * (sp + 1)],
                                       start=(i == 0), stop=(i == 11))
                return ins
            p.emit("T", mm, rd=[xt, qpw_sb], wr=[psq])
            p.emit("A", lambda e, qft=qft: e.copy(qft[:], psq[:]), rd=[psq], wr=[qft])
            p.cur_src = qft
            p.cur_dst = qbt
            p.cs_buf = cs_sb
            emit_rope_tm(p, qbt[:], qft[:], cs_sb, t, 32, rt1, rt2)
            for g0 in (0, 8):
                sg = stg[(t + g0 // 8) % 2]

                def tr(e, qbt=qbt, g0=g0):
                    ins = None
                    for u in range(8):
                        ins = e.matmul(psT[:, 128 * u:128 * (u + 1)],
                                       qbt[:, 128 * (g0 + u):128 * (g0 + u + 1)],
                                       identb[:], start=True, stop=True)
                    return ins
                p.emit("T", tr, rd=[qbt, identb], wr=[psT])
                p.emit("V", lambda e, sg=sg: e.tensor_copy(sg[:], psT[:]), rd=[psT], wr=[sg])
                p.emit("GD", lambda e, sg=sg, g0=g0, t=t: e.dma_start(
                    out=qpeT_v[:, g0:g0 + 8, 128 * t:128 * (t + 1)],
                    in_=sg[:].rearrange("d (c t) -> d c t", t=128)),
                    rd=[sg], wr=[hbm["qpeT_h"].box(ps=(128 * g0, 128 * (g0 + 8)),
                                                   fs=(128 * t, 128 * (t + 1)))])

    with p.phase("l0idx"):
        ikT_sb = p.sbuf("l0iks", [128, S], BF16)
        p.emit("S", lambda e: e.dma_start(out=ikT_sb[:], in_=hbm["ikT_h"].ap()),
               rd=[hbm["ikT_h"]], wr=[ikT_sb])
        iqT_v = hbm["iqT_h"].ap().rearrange("(c d) t -> d c t", d=128)
        iqs = [p.sbuf("l0iqs", [128, 16 * 128], BF16) for _ in range(2)]
        psS = [p.psum("l0isp", [128, S], F32) for _ in range(2)]
        relu = [p.sbuf("l0irl", [128, S], BF16) for _ in range(2)]
        acc = [p.sbuf("l0iac", [128, S], F32) for _ in range(2)]
        msk = [p.sbuf("l0imk", [128, S], BF16) for _ in range(2)]
        dumpb = p.sbuf("l0idm", [128, S], BF16)
        negbig = p.sbuf("l0inb", [128, S], F32)
        cols = p.sbuf("l0icl", [128, 8], F32)
        p.emit("V", lambda e: e.memset(negbig[:], NEG_BIG), wr=[negbig])
        for t in range(NT):
            W = 128 * (t + 1)
            iq = iqs[t % 2]
            ac = acc[t % 2]
            mk = msk[t % 2]
            p.emit("S", lambda e, iq=iq, t=t: e.dma_start(
                out=iq[:], in_=iqT_v[:, :, 128 * t:128 * (t + 1)]),
                rd=[hbm["iqT_h"]], wr=[iq])
            for n in range(16):
                R = psS[n % 2]
                rl = relu[n % 2]

                def mm(e, iq=iq, n=n, R=R, W=W):
                    ins = None
                    for sp in range(ceil_div(W, 512)):
                        w = min(512, W - 512 * sp)
                        ins = e.matmul(R[:, 512 * sp:512 * sp + w], iq[:, 128 * n:128 * (n + 1)],
                                       ikT_sb[:, 512 * sp:512 * sp + w], start=True, stop=True)
                    return ins
                p.emit("T", mm, rd=[iq.box(fs=(128 * n, 128 * (n + 1))), ikT_sb], wr=[R.box(fs=(0, W))])
                p.emit("A", lambda e, R=R, rl=rl, W=W: e.activation(
                    rl[:, 0:W], R[:, 0:W], AF.Relu, scale=IDX_SC),
                    rd=[R.box(fs=(0, W))], wr=[rl.box(fs=(0, W))])
                iwcol = iw_sb[:, 16 * t + n:16 * t + n + 1]
                if n == 0:
                    p.emit("V", lambda e, rl=rl, ac=ac, W=W, iwcol=iwcol: e.tensor_single_scalar(
                        ac[:, 0:W], rl[:, 0:W], iwcol, ALU.mult),
                        rd=[rl.box(fs=(0, W)), iw_sb.box(fs=(16 * t, 16 * t + 16))],
                        wr=[ac.box(fs=(0, W))])
                else:
                    p.emit("V", lambda e, rl=rl, ac=ac, W=W, iwcol=iwcol: e.scalar_tensor_tensor(
                        ac[:, 0:W], rl[:, 0:W], iwcol, ac[:, 0:W], ALU.mult, ALU.add),
                        rd=[rl.box(fs=(0, W)), ac.box(fs=(0, W)),
                            iw_sb.box(fs=(16 * t, 16 * t + 16))],
                        wr=[ac.box(fs=(0, W))])
            p.emit("V", lambda e, ac=ac, W=W: e.reduce_max(cols[:, 1:2], ac[:, 0:W], axis=AX.X),
                   rd=[ac.box(fs=(0, W))], wr=[cols.box(fs=(1, 2))])
            p.emit("V", lambda e, ac=ac, W=W: e.tensor_reduce(cols[:, 0:1], ac[:, 0:W], AX.X, ALU.min),
                   rd=[ac.box(fs=(0, W))], wr=[cols.box(fs=(0, 1))])
            p.emit("V", lambda e, ac=ac, t=t: e.tensor_add(
                ac[:, 128 * t:128 * (t + 1)], ac[:, 128 * t:128 * (t + 1)], tri_sb[:]),
                rd=[ac.box(fs=(128 * t, W)), tri_sb], wr=[ac.box(fs=(128 * t, W))])
            if t >= 8:
                p.emit("V", lambda e: e.tensor_scalar(cols[:, 2:3], cols[:, 0:1], -1.0, None, ALU.add),
                       rd=[cols.box(fs=(0, 1))], wr=[cols.box(fs=(2, 3))])
                p.emit("V", lambda e: e.tensor_scalar(cols[:, 3:4], cols[:, 1:2], 1.0, None, ALU.add),
                       rd=[cols.box(fs=(1, 2))], wr=[cols.box(fs=(3, 4))])
                thr = float(2 * 1024 - W)
                for it in range(26):
                    p.emit("V", lambda e: e.tensor_add(cols[:, 4:5], cols[:, 2:3], cols[:, 3:4]),
                           rd=[cols.box(fs=(2, 4))], wr=[cols.box(fs=(4, 5))])
                    p.emit("V", lambda e: e.tensor_scalar_mul(cols[:, 5:6], cols[:, 4:5], 0.5),
                           rd=[cols.box(fs=(4, 5))], wr=[cols.box(fs=(5, 6))])
                    p.emit("V", lambda e: e.tensor_scalar_mul(cols[:, 6:7], cols[:, 4:5], -0.5),
                           rd=[cols.box(fs=(4, 5))], wr=[cols.box(fs=(6, 7))])
                    p.emit("A", lambda e, ac=ac, W=W: e.activation(
                        dumpb[:, 0:W], ac[:, 0:W], AF.Sign, bias=cols[:, 6:7],
                        accum_out=cols[:, 7:8]),
                        rd=[ac.box(fs=(0, W)), cols.box(fs=(6, 7))],
                        wr=[dumpb.box(fs=(0, W)), cols.box(fs=(7, 8))])
                    p.emit("V", lambda e, thr=thr: e.tensor_single_scalar(
                        u8c[:], cols[:, 7:8], thr, ALU.is_ge),
                        rd=[cols.box(fs=(7, 8))], wr=[u8buf])
                    p.emit("V", lambda e: e.select(cols[:, 2:3], u8c[:], cols[:, 5:6], cols[:, 2:3]),
                           rd=[u8buf, cols.box(fs=(2, 3)), cols.box(fs=(5, 6))],
                           wr=[cols.box(fs=(2, 3))])
                    p.emit("V", lambda e: e.select(cols[:, 3:4], u8c[:], cols[:, 3:4], cols[:, 5:6]),
                           rd=[u8buf, cols.box(fs=(3, 4)), cols.box(fs=(5, 6))],
                           wr=[cols.box(fs=(3, 4))])
            else:
                p.emit("V", lambda e: e.memset(cols[:, 2:3], -1e29), wr=[cols.box(fs=(2, 3))])
            p.emit("V", lambda e, ac=ac, mk=mk, W=W: e.scalar_tensor_tensor(
                mk[:, 0:W], ac[:, 0:W], cols[:, 2:3], negbig[:, 0:W], ALU.is_lt, ALU.mult),
                rd=[ac.box(fs=(0, W)), cols.box(fs=(2, 3)), negbig], wr=[mk.box(fs=(0, W))])
            p.emit("GD", lambda e, mk=mk, t=t, W=W: e.dma_start(
                out=hbm["M_h"][128 * t:128 * (t + 1), 0:W], in_=mk[:, 0:W]),
                rd=[mk.box(fs=(0, W))], wr=[hbm["M_h"].box(ps=(128 * t, 128 * (t + 1)))])

    with p.phase("l0at"):
        qlT_sb = p.sbuf("l0ql", [128, 12 * S], BF16)
        M_sb = p.sbuf("l0M", [128, NT * S], BF16)
        kvT_sb = p.sbuf("l0kv", [128, 4 * S], BF16)
        kpeT_sb = p.sbuf("l0kp", [64, S], BF16)
        qlT_v = hbm["qlatT_h"].ap().rearrange("(c d) t -> d c t", d=128)
        kvT_v = hbm["kvnT_h"].ap().rearrange("(c d) t -> d c t", d=128)
        p.emit("S", lambda e: e.dma_start(
            out=qlT_sb[:].rearrange("p (c t) -> p c t", c=12), in_=qlT_v[:]),
            rd=[hbm["qlatT_h"]], wr=[qlT_sb])
        p.emit("S", lambda e: e.dma_start(
            out=kvT_sb[:].rearrange("p (c t) -> p c t", c=4), in_=kvT_v[:]),
            rd=[hbm["kvnT_h"]], wr=[kvT_sb])
        p.emit("S", lambda e: e.dma_start(out=kpeT_sb[:], in_=hbm["kpeT_h"].ap()),
               rd=[hbm["kpeT_h"]], wr=[kpeT_sb])
        Mv = hbm["M_h"].ap().rearrange("(t p) k -> p t k", p=128)
        p.emit("S", lambda e: e.dma_start(
            out=M_sb[:].rearrange("p (t k) -> p t k", t=NT), in_=Mv[:]),
            rd=[hbm["M_h"]], wr=[M_sb])
        kbk = [p.sbuf("l0bk", [128, 4 * 128], BF16) for _ in range(2)]
        kbv = [p.sbuf("l0bv", [128, 4 * 128], BF16) for _ in range(2)]
        qbn = [p.sbuf("l0bq", [128, 12 * 128], BF16) for _ in range(2)]
        K_sb = [p.sbuf("l0K", [128, S], BF16) for _ in range(2)]
        v_sb = [p.sbuf("l0V", [128, S], BF16) for _ in range(2)]
        qn_sb = [p.sbuf("l0qn", [128, S], BF16) for _ in range(2)]
        qpe_sb = [p.sbuf("l0qp", [64, S], BF16) for _ in range(2)]
        P_sb = [p.sbuf("l0P", [128, S], BF16) for _ in range(2)]
        PT_sb = [p.sbuf("l0PT", [128, S], BF16) for _ in range(2)]
        stage = [p.sbuf("l0sg", [128, S], BF16) for _ in range(2)]
        psA = [p.psum("l0pA", [128, S], F32) for _ in range(2)]
        cols = p.sbuf("l0ac", [128, 8], F32)
        kbk_v = ins_d["kvbk"].ap().rearrange("(c p) o -> p c o", p=128)
        kbv_v = ins_d["kvbv"].ap().rearrange("(c p) o -> p c o", p=128)
        qbn_v = ins_d["qbn"].ap().rearrange("(c p) o -> p c o", p=128)
        for h in range(32):
            hb = h % 2
            Kb, vb, qnb, qpb = K_sb[hb], v_sb[hb], qn_sb[hb], qpe_sb[hb]
            R0, R1 = psA[0], psA[1]
            p.emit("GD", lambda e, h=h, hb=hb: e.dma_start(
                out=kbk[hb][:], in_=kbk_v[:, :, 128 * h:128 * (h + 1)]),
                rd=[ins_d["kvbk"]], wr=[kbk[hb]])
            p.emit("GD", lambda e, h=h, hb=hb: e.dma_start(
                out=kbv[hb][:], in_=kbv_v[:, :, 128 * h:128 * (h + 1)]),
                rd=[ins_d["kvbv"]], wr=[kbv[hb]])
            p.emit("GD", lambda e, h=h, hb=hb: e.dma_start(
                out=qbn[hb][:], in_=qbn_v[:, :, 128 * h:128 * (h + 1)]),
                rd=[ins_d["qbn"]], wr=[qbn[hb]])
            p.emit("S", lambda e, h=h, qpb=qpb: e.dma_start(
                out=qpb[:], in_=hbm["qpeT_h"][64 * h:64 * (h + 1), :]),
                rd=[hbm["qpeT_h"].box(ps=(64 * h, 64 * (h + 1)))], wr=[qpb])
            def mmk(e, hb=hb):
                ins = None
                for r in range(4):
                    for sp in range(4):
                        ins = e.matmul(R0[:, 512 * sp:512 * (sp + 1)],
                                       kbk[hb][:, 128 * r:128 * (r + 1)],
                                       kvT_sb[:, S * r + 512 * sp:S * r + 512 * (sp + 1)],
                                       start=(r == 0), stop=(r == 3))
                return ins
            p.emit("T", mmk, rd=[kbk[hb], kvT_sb], wr=[R0])
            p.emit("A", lambda e, Kb=Kb: e.copy(Kb[:], R0[:]), rd=[R0], wr=[Kb])
            def mmv(e, hb=hb):
                ins = None
                for j in range(16):
                    for r in range(4):
                        ins = e.matmul(R1[:, 128 * j:128 * (j + 1)],
                                       kvT_sb[:, S * r + 128 * j:S * r + 128 * (j + 1)],
                                       kbv[hb][:, 128 * r:128 * (r + 1)],
                                       start=(r == 0), stop=(r == 3))
                return ins
            p.emit("T", mmv, rd=[kbv[hb], kvT_sb], wr=[R1])
            p.emit("A", lambda e, vb=vb: e.copy(vb[:], R1[:]), rd=[R1], wr=[vb])
            def mmq(e, hb=hb):
                ins = None
                for r in range(12):
                    for sp in range(4):
                        ins = e.matmul(R0[:, 512 * sp:512 * (sp + 1)],
                                       qbn[hb][:, 128 * r:128 * (r + 1)],
                                       qlT_sb[:, S * r + 512 * sp:S * r + 512 * (sp + 1)],
                                       start=(r == 0), stop=(r == 11))
                return ins
            p.emit("T", mmq, rd=[qbn[hb], qlT_sb], wr=[R0])
            p.emit("A", lambda e, qnb=qnb: e.copy(qnb[:], R0[:]), rd=[R0], wr=[qnb])
            for t in range(NT):
                W = 128 * (t + 1)
                par = t % 2
                RS, RT = psA[par], psA[1 - par]
                Pp, PTp, st = P_sb[par], PT_sb[par], stage[hb]
                nsp = ceil_div(W, 512)

                def mms(e, qnb=qnb, qpb=qpb, Kb=Kb, t=t, W=W, RS=RS, nsp=nsp):
                    ins = None
                    for sp in range(nsp):
                        w = min(512, W - 512 * sp)
                        ins = e.matmul(RS[:, 512 * sp:512 * sp + w], qnb[:, 128 * t:128 * (t + 1)],
                                       Kb[:, 512 * sp:512 * sp + w], start=True, stop=False)
                    for sp in range(nsp):
                        w = min(512, W - 512 * sp)
                        ins = e.matmul(RS[:, 512 * sp:512 * sp + w], qpb[0:64, 128 * t:128 * (t + 1)],
                                       kpeT_sb[0:64, 512 * sp:512 * sp + w], start=False, stop=False)
                    for sp in range(nsp):
                        w = min(512, W - 512 * sp)
                        ins = e.matmul(RS[:, 512 * sp:512 * sp + w], identb[:],
                                       M_sb[:, S * t + 512 * sp:S * t + 512 * sp + w],
                                       start=False, stop=True)
                    return ins
                p.emit("T", mms, rd=[qnb.box(fs=(128 * t, 128 * (t + 1))), Kb,
                                     qpb.box(fs=(128 * t, 128 * (t + 1))), kpeT_sb,
                                     identb, M_sb.box(fs=(S * t, S * t + W))],
                       wr=[RS.box(fs=(0, W))])
                p.emit("V", lambda e, RS=RS, W=W: e.reduce_max(cols[:, 0:1], RS[:, 0:W],
                                                               axis=AX.X, negate=True),
                       rd=[RS.box(fs=(0, W))], wr=[cols.box(fs=(0, 1))])
                p.emit("V", lambda e: e.tensor_scalar_mul(cols[:, 1:2], cols[:, 0:1], QKD_SC),
                       rd=[cols.box(fs=(0, 1))], wr=[cols.box(fs=(1, 2))])
                p.emit("A", lambda e, RS=RS, Pp=Pp, W=W: e.activation(
                    Pp[:, 0:W], RS[:, 0:W], AF.Exp, bias=cols[:, 1:2], scale=QKD_SC,
                    accum_out=cols[:, 2:3]),
                    rd=[RS.box(fs=(0, W)), cols.box(fs=(1, 2))],
                    wr=[Pp.box(fs=(0, W)), cols.box(fs=(2, 3))])
                p.emit("V", lambda e: e.reciprocal(cols[:, 3:4], cols[:, 2:3]),
                       rd=[cols.box(fs=(2, 3))], wr=[cols.box(fs=(3, 4))])
                p.emit("V", lambda e, Pp=Pp, W=W: e.tensor_scalar_mul(Pp[:, 0:W], Pp[:, 0:W],
                                                                      cols[:, 3:4]),
                       rd=[Pp.box(fs=(0, W)), cols.box(fs=(3, 4))], wr=[Pp.box(fs=(0, W))])

                def mmpt(e, Pp=Pp, RT=RT, W=W):
                    ins = None
                    for j in range(W // 128):
                        ins = e.matmul(RT[:, 128 * j:128 * (j + 1)], Pp[:, 128 * j:128 * (j + 1)],
                                       identb[:], start=True, stop=True)
                    return ins
                p.emit("T", mmpt, rd=[Pp.box(fs=(0, W)), identb], wr=[RT.box(fs=(0, W))])
                p.emit("A", lambda e, RT=RT, PTp=PTp, W=W: e.copy(PTp[:, 0:W], RT[:, 0:W]),
                       rd=[RT.box(fs=(0, W))], wr=[PTp.box(fs=(0, W))])

                def mmav(e, PTp=PTp, vb=vb, RS=RS, W=W):
                    ins = None
                    for j in range(W // 128):
                        ins = e.matmul(RS[:, 0:128], vb[:, 128 * j:128 * (j + 1)],
                                       PTp[:, 128 * j:128 * (j + 1)],
                                       start=(j == 0), stop=(j == W // 128 - 1))
                    return ins
                p.emit("T", mmav, rd=[PTp.box(fs=(0, W)), vb], wr=[RS.box(fs=(0, 512))])
                p.emit("A", lambda e, RS=RS, st=st, t=t: e.copy(st[:, 128 * t:128 * (t + 1)],
                                                                RS[:, 0:128]),
                       rd=[RS.box(fs=(0, 512))], wr=[st.box(fs=(128 * t, 128 * (t + 1)))])
            p.emit("S", lambda e, st=stage[hb], h=h: e.dma_start(
                out=hbm["attnT_h"][128 * h:128 * (h + 1), :], in_=st[:]),
                rd=[stage[hb]], wr=[hbm["attnT_h"].box(ps=(128 * h, 128 * (h + 1)))])

    emit_proj_from_h(p, hbm["attnT_h"], ins_d["ow"], sub_h, tag="l0o")


def build_l0attn(out_kind="ExternalOutput", debug=False):
    nc = bass.Bass("TRN2", target_bir_lowering=False, debug=False)
    with ExitStack() as ctx:
        p = Prog(nc, ctx)
        x_in = p.io("x_in", [S, D], F32, "ExternalInput")
        names = {"qa": ("qa", [D, 1536]), "qanw": ("qanw", [128, 1536]),
                 "kva": ("kva", [D, 576]), "kvnw": ("kvnw", [128, 512]),
                 "qbn": ("qbn", [1536, D]), "qbpe": ("qbpe", [1536, 2048]),
                 "kvbk": ("kvbk", [512, D]), "kvbv": ("kvbv", [512, D]),
                 "ow": ("ow", [D, D]), "idxq": ("idxq", [D, 2048]),
                 "idxk": ("idxk", [D, 128]), "idxw": ("idxw", [D, 16])}
        ins_d = {k: p.io(nm, shp, F32, "ExternalInput") for k, (nm, shp) in names.items()}
        identd = p.io("identd", [128, 128], F32, "ExternalInput")
        trid = p.io("trid", [128, 128], F32, "ExternalInput")
        csd = p.io("csd", [S, 64], F32, "ExternalInput")
        x_out = p.io("x_out", [S, D], F32, out_kind)
        hbm = {"xT_h": p.dram("xT_h", [D, S], BF16),
               "qlatT_h": p.dram("qlatT_h", [1536, S], BF16),
               "kvnT_h": p.dram("kvnT_h", [512, S], BF16),
               "kpeT_h": p.dram("kpeT_h", [64, S], BF16),
               "ikT_h": p.dram("ikT_h", [128, S], BF16),
               "iqT_h": p.dram("iqT_h", [2048, S], BF16),
               "qpeT_h": p.dram("qpeT_h", [2048, S], BF16),
               "M_h": p.dram("M_h", [S, S], BF16),
               "attnT_h": p.dram("attnT_h", [D, S], BF16)}
        sub_h = p.dram("sub_h", [S, D], BF16)
        identb = p.sbuf("identb", [128, 128], BF16)
        tri_sb = p.sbuf("tri_sb", [128, 128], F32)
        cs_sb = p.sbuf("cs_sb", [128, NT * 64], F32)
        global u8buf, u8c
        u8buf = p.sbuf("u8b", [128, 1], U8)
        u8c = u8buf[:]
        p.emit("GD", lambda e: e.dma_start(out=identb[:], in_=identd.ap()), rd=[identd], wr=[identb])
        p.emit("AD", lambda e: e.dma_start(out=tri_sb[:], in_=trid.ap()), rd=[trid], wr=[tri_sb])
        p.emit("AD", lambda e: e.dma_start(
            out=cs_sb[:].rearrange("p (t f) -> p t f", t=NT),
            in_=csd.ap().rearrange("(t p) f -> p t f", p=128)), rd=[csd], wr=[cs_sb])
        emit_l0_attn(p, x_in, ins_d, hbm, identb, tri_sb, cs_sb, sub_h)
        if debug:
            p.tap(hbm["qlatT_h"], "d_qlatT"); p.tap(hbm["kvnT_h"], "d_kvnT")
            p.tap(hbm["kpeT_h"], "d_kpeT"); p.tap(hbm["iqT_h"], "d_iqT")
            p.tap(hbm["qpeT_h"], "d_qpeT"); p.tap(hbm["M_h"], "d_M")
            p.tap(hbm["attnT_h"], "d_attnT"); p.tap(sub_h, "d_sub")
        emit_ln_residual(p, sub_h, x_in, x_out)
        p.finalize()
    return nc


def _l0attn_maps(x_full, inputs):
    qb = inputs["L0_attn_q_b_w"].reshape(1536, 32, 192)
    kvb = inputs["L0_attn_kv_b_w"].reshape(512, 32, 256)
    pos = np.arange(S, dtype=np.float32)
    inv = 1.0 / 10000.0 ** (np.arange(0, 64, 2, dtype=np.float32) / 64.0)
    ang = pos[:, None] * inv[None, :]
    csd = np.concatenate([np.cos(ang), np.sin(ang)], 1).astype(np.float32)
    tri = np.where(np.arange(128)[:, None] >= np.arange(128)[None, :], 0.0, NEG_BIG).astype(np.float32)
    base = {
        "qa": inputs["L0_attn_q_a_w"],
        "qanw": np.ascontiguousarray(np.broadcast_to(inputs["L0_attn_q_a_norm_w"], (128, 1536))),
        "kva": inputs["L0_attn_kv_a_w"],
        "kvnw": np.ascontiguousarray(np.broadcast_to(inputs["L0_attn_kv_a_norm_w"], (128, 512))),
        "qbn": np.ascontiguousarray(qb[:, :, :128].reshape(1536, D)),
        "qbpe": np.ascontiguousarray(qb[:, :, 128:].reshape(1536, 2048)),
        "kvbk": np.ascontiguousarray(kvb[:, :, :128].reshape(512, D)),
        "kvbv": np.ascontiguousarray(kvb[:, :, 128:].reshape(512, D)),
        "ow": inputs["L0_attn_o_w"],
        "idxq": inputs["L0_attn_idx_wq_w"],
        "idxk": inputs["L0_attn_idx_wk_w"],
        "idxw": inputs["L0_attn_idx_weights_w"],
        "identd": np.eye(128, dtype=np.float32),
        "trid": tri, "csd": csd,
    }
    return [dict(base, x_in=np.ascontiguousarray(x_full[c])) for c in range(NCORES)]


def run_l0attn(x_full, inputs, debug=False):
    nc = build_l0attn(debug=debug)
    res = run_bass_kernel_spmd(nc, _l0attn_maps(x_full, inputs), core_ids=list(range(NCORES)))
    if debug:
        return res
    return np.stack([r["x_out"] for r in res.results], 0)



def build_dora(out_kind="ExternalOutput", debug=False):
    nc = bass.Bass("TRN2", target_bir_lowering=False, debug=False)
    with ExitStack() as ctx:
        p = Prog(nc, ctx)
        x_in = p.io("x_in", [S, D], F32, "ExternalInput")
        wt = p.io("wt", [D, D], F32, "ExternalInput")
        at = p.io("at", [D, 32], F32, "ExternalInput")
        btt = p.io("btt", [32, D], F32, "ExternalInput")
        mag = p.io("mag", [1, D], F32, "ExternalInput")
        identd = p.io("identd", [128, 128], F32, "ExternalInput")
        x_out = p.io("x_out", [S, D], F32, out_kind)
        xT_h = p.dram("xT_h", [D, S], BF16)
        sub_h = p.dram("sub_h", [S, D], BF16)
        identb = p.sbuf("identb", [128, 128], BF16)
        p.emit("GD", lambda e: e.dma_start(out=identb[:], in_=identd.ap()), rd=[identd], wr=[identb])
        emit_dora(p, x_in, wt, at, btt, mag, xT_h, sub_h, identb)
        if debug:
            p.tap(xT_h, "d_xT")
            p.tap(sub_h, "d_sub")
            p.tap_sbuf(p.dbg_c_row, "d_crow")
            p.tap_sbuf(p.dbg_xa, "d_xa")
            p.tap_sbuf(p.dbg_crep, "d_crep")
        emit_ln_residual(p, sub_h, x_in, x_out)
        p.finalize()
    return nc


def _dora_maps(x_full, inputs):
    wt = np.ascontiguousarray(inputs["L1_ffn_W"].T)
    at = np.ascontiguousarray(inputs["L1_ffn_lora_A"].T)
    btt = np.ascontiguousarray(inputs["L1_ffn_lora_B"].T)
    mag = np.ascontiguousarray(inputs["L1_ffn_magnitude"].reshape(1, D))
    identd = np.eye(128, dtype=np.float32)
    return [{"x_in": np.ascontiguousarray(x_full[c]), "wt": wt, "at": at,
             "btt": btt, "mag": mag, "identd": identd} for c in range(NCORES)]


def run_dora(x_full, inputs, debug=False):
    nc = build_dora(debug=debug)
    res = run_bass_kernel_spmd(nc, _dora_maps(x_full, inputs), core_ids=list(range(NCORES)))
    if debug:
        return res
    return np.stack([r["x_out"] for r in res.results], 0)


_PROG_CACHE = {}


def _get_prog(name, builder):
    if name not in _PROG_CACHE:
        _PROG_CACHE[name] = builder()
    return _PROG_CACHE[name]


def kernel(**inputs):
    x = inputs["L0_attn_hidden_states"].astype(np.float32, copy=False)
    nc0 = _get_prog("l0attn", build_l0attn)
    res = run_bass_kernel_spmd(nc0, _l0attn_maps(x, inputs), core_ids=list(range(NCORES)))
    x = np.stack([r["x_out"] for r in res.results], 0)
    nc1 = _get_prog("l0ffn", build_l0ffn)
    res = run_bass_kernel_spmd(nc1, _l0ffn_maps(x, inputs), core_ids=list(range(NCORES)))
    x = np.stack([r["x_out"] for r in res.results], 0)
    nc2 = _get_prog("l1attn", build_l1attn)
    res = run_bass_kernel_spmd(nc2, _l1attn_maps(x, inputs), core_ids=list(range(NCORES)))
    x = np.stack([r["x_out"] for r in res.results], 0)
    nc3 = _get_prog("dora", build_dora)
    res = run_bass_kernel_spmd(nc3, _dora_maps(x, inputs), core_ids=list(range(NCORES)))
    return np.stack([r["x_out"] for r in res.results], 0)
